# version 02
# baseline (speedup 1.0000x reference)
import sys
import numpy as np
from contextlib import ExitStack

try:
    import concourse.bass as bass
except ImportError:
    sys.path.insert(0, "/opt/trn_rl_repo")
    import concourse.bass as bass
import concourse.mybir as mybir
from concourse.bass_utils import run_bass_kernel_spmd

F32 = mybir.dt.float32
BF16 = mybir.dt.bfloat16
AF = mybir.ActivationFunctionType
ALU = mybir.AluOpType
AX = mybir.AxisListType

HID = 4096
NKB = 32
INTER = 8192
NB_GATE, NB_HBC, NB_FFN = 64, 80, 32
NCB = 179
CB_DT, CB_LORA, CB_RT = 144, 145, 146
CB_ORDER = [CB_RT, CB_DT, CB_LORA] + list(range(0, 144)) + list(range(147, 179))
KB_D = 65
ND = 32
KW = 4
LN_EPS = 1e-5
RMS_EPS = 1e-5
Q_LO, Q_HI = 611, 16383
NQ = Q_HI - Q_LO
LORA_SCALE = 2.0
NCORE = 8
SEQ = 16384
STOP_AFTER = None


class TSem:
    def __init__(self, es, nc, name):
        self.h = es.enter_context(nc.semaphore(name))
        self.n = 0

    def inc(self, ins, k=1):
        ins.then_inc(self.h, k)
        self.n += k
        return ins


def build(tok=2048, taps=()):
    T = tok + 3
    NCH = tok // 128
    TTS = []
    r = T
    while r > 0:
        TTS.append(min(411, r))
        r -= TTS[-1]
    NTT = len(TTS)
    TTM = max(TTS)
    TSW = min(1024, tok)
    NTS = tok // TSW
    NSUB = TSW // 512

    nc = bass.Bass("TRN2", target_bir_lowering=False,
                   detect_race_conditions=False)

    def din(name, shape, dt=F32):
        return nc.dram_tensor(name, shape, dt, kind="ExternalInput").ap()

    x_fm = din("x", [HID, T])
    wa = din("wa", [HID, NCB * 128])
    wd = din("wd", [KB_D * 128, HID])
    cw = din("cw", [128, NB_HBC * KW])
    cbv = din("cb", [128, NB_HBC])
    dtb = din("dtb", [128, 1])
    alog = din("alog", [128, 1])
    dvec = din("dv", [128, 1])
    nwv = din("nw", [1, INTER])
    bpv = din("bp", [128, ND])
    rbv = din("rb", [1, 8])
    rmask = din("rmask", [8, T])
    lmask = din("lmask", [128, 8])
    ilmask = din("ilmask", [128, 8])

    out = nc.dram_tensor("out", [ND, 128, tok], F32, kind="ExternalOutput").ap()

    def scr(name, shape, dt):
        kind = "ExternalOutput" if name in taps else "Internal"
        return nc.dram_tensor(name, shape, dt, kind=kind).ap()

    d_gate = scr("d_gate", [NB_GATE, 128, T], BF16)
    d_hbc = scr("d_hbc", [NB_HBC, 128, T], BF16)
    d_ffnb = scr("d_ffnb", [NB_FFN, 128, T], BF16)
    d_yloc = scr("d_yloc", [NCH, 128, INTER], F32)
    d_norm = scr("d_norm", [NCH, 128, INTER], BF16)
    d_acs = scr("d_acs", [NCH, 128, 128], F32)
    d_rstd = scr("d_rstd", [1, T], F32)
    d_mr = scr("d_mr", [1, T], F32)
    d_cd = scr("d_cd", [128, 1], F32)
    d_dtot = scr("d_dtot", [128, 1], F32)
    d_dst = scr("d_dst", [128, 1], F32)
    d_rt = scr("d_rt", [1, 8], F32)
    d_sbf = scr("d_sbf", [128, INTER], BF16) if "d_sbf" in taps else None
    d_ebf = scr("d_ebf", [128, INTER], BF16) if "d_ebf" in taps else None
    d_cco = scr("d_cco", [NCORE * 128, INTER + 1], BF16) if "d_cco" in taps else None
    cc_in = nc.dram_tensor("cc_in", [128, INTER + 1], BF16, kind="Internal").ap()
    cc_out = nc.dram_tensor("cc_out", [NCORE * 128, INTER + 1], BF16,
                            kind="Internal", addr_space="Shared").ap()
    ar_in = nc.dram_tensor("ar_in", [1, 8], F32, kind="Internal").ap()
    ar_out = nc.dram_tensor("ar_out", [1, 8], F32, kind="Internal").ap()

    def bc_ap(dap, npart, nfree):
        return bass.AP(dap.tensor, 0, [[0, npart], [1, nfree]])

    top = ExitStack()
    with top:
        et = top.enter_context

        dt_raw = et(nc.sbuf_tensor("dt_raw", [128, T], F32))
        t_sc = et(nc.sbuf_tensor("t_sc", [128, T], BF16))
        cdlog = [et(nc.sbuf_tensor(f"cdlog{i}", [128, 1], F32)) for i in range(2)]
        ident_bf = et(nc.sbuf_tensor("ident_bf", [128, 128], BF16))
        ident_f = et(nc.sbuf_tensor("ident_f", [128, 128], F32))
        trimask = et(nc.sbuf_tensor("trimask", [128, 128], F32))
        ones128 = et(nc.sbuf_tensor("ones128", [128, 128], F32))
        cw_bf = et(nc.sbuf_tensor("cw_bf", [128, NB_HBC * KW], BF16))
        cb_bf = et(nc.sbuf_tensor("cb_bf", [128, NB_HBC], BF16))
        dtb_t = et(nc.sbuf_tensor("dtb_t", [128, 1], F32))
        acol = et(nc.sbuf_tensor("acol", [128, 1], F32))
        alog_t = et(nc.sbuf_tensor("alog_t", [128, 1], F32))
        dbc = et(nc.sbuf_tensor("dbc", [128, 128], F32))
        bp_t = et(nc.sbuf_tensor("bp_t", [128, ND], F32))
        rb_t = et(nc.sbuf_tensor("rb_t", [1, 8], F32))
        lm_t = et(nc.sbuf_tensor("lm_t", [128, 8], F32))
        ilm_t = et(nc.sbuf_tensor("ilm_t", [128, 8], F32))
        rr_col = et(nc.sbuf_tensor("rr_col", [128, 1], F32))
        onecol = et(nc.sbuf_tensor("onecol", [128, 1], BF16))

        pYd = et(nc.psum_tensor("pYd", [128, 1024], F32))
        pYo = et(nc.psum_tensor("pYo", [128, 1024], F32))
        pF = et(nc.psum_tensor("pF", [128, 1024], F32))
        pAux = et(nc.psum_tensor("pAux", [128, 1024], F32))

        SN = ("bar s_sml s_x s_sq s_smm s_stats s_stev s_ln1 s_lna s_lnro "
              "s_lnbc s_ln2 s_xn s_w s_mm s_evv s_eva s_st s_rt s_ari s_cc s_aro "
              "s_rt2 s_rt3 s_rt4 s_rt5 s_setup "
              "s_hbcl s_conv s_silu s_dt1 s_dta s_acs s_cdu s_acso s_trnA "
              "s_trevA s_trnH s_trevH s_gmt s_gmm s_cde s_dtt s_sm1 s_sm2 "
              "s_sm3 s_sm4 s_dec s_sdr s_sdo s_rbc s_mtd s_mte s_mt s_bd "
              "s_yd s_yo s_f s_yev s_sup s_yst "
              "s_blob s_cc2 s_gt s_dstg s_dbc s_ecomb "
              "s_ylc s_glc s_gtr s_gte s_corr s_cev s_rms1 s_rmsa s_nrm "
              "s_nst s_rhs s_wd s_wl s_fb s_xb s_dmm s_dev s_out s_nwl s_w1 s_wd1 "
              "s_lnr2 s_tap s_gt1").split()
        S = {sn: TSem(top, nc, sn) for sn in SN}

        NENG = 5
        phase = [0]

        def bar_in(eng):
            if phase[0] > 0:
                eng.wait_ge(S["bar"].h, NENG * phase[0])

        def bar_out(eng):
            S["bar"].inc(eng.nop())

        ev_sched = []
        _cv, _ca = [0], [0]
        for cb in CB_ORDER:
            for tt in range(NTT):
                if 0 <= cb < NB_GATE:
                    _ca[0] += 1
                    ev_sched.append(("s_eva", _ca[0]))
                else:
                    _cv[0] += 1
                    ev_sched.append(("s_evv", _cv[0]))
        staged = [cb for cb in CB_ORDER if cb < 144 or cb >= 147]
        st_idx = {cb: i for i, cb in enumerate(staged)}
        CB_POS = {cb: i for i, cb in enumerate(CB_ORDER)}

        def pass_a():
            with ExitStack() as sA:
                ea = sA.enter_context
                xnreg = ea(nc.sbuf_tensor("xnreg", [128, NKB * T], BF16))
                wbuf = [ea(nc.sbuf_tensor(f"wbuf{i}", [128, NKB * 128], BF16))
                        for i in range(2)]
                stage = ea(nc.sbuf_tensor("stg", [128, T], BF16))
                x2t = [ea(nc.sbuf_tensor(f"x2t{i}", [128, TTM], BF16))
                       for i in range(2)]
                r_s1 = ea(nc.sbuf_tensor("r_s1", [1, TTM], F32))
                r_s2 = ea(nc.sbuf_tensor("r_s2", [1, TTM], F32))
                r_m = ea(nc.sbuf_tensor("r_m", [1, TTM], F32))
                r_t = ea(nc.sbuf_tensor("r_t", [1, TTM], F32))
                r_rstd = ea(nc.sbuf_tensor("r_rstd", [1, TTM], F32))
                r_mr = ea(nc.sbuf_tensor("r_mr", [1, TTM], F32))
                rbc_bf = ea(nc.sbuf_tensor("rbc_bf", [128, T], BF16))
                mrb_bf = ea(nc.sbuf_tensor("mrb_bf", [128, T], BF16))
                nstg = ea(nc.sbuf_tensor("nstg", [128, T], BF16))
                tmp8 = ea(nc.sbuf_tensor("tmp8", [8, TTM], F32))
                red8 = ea(nc.sbuf_tensor("red8", [8, 1], F32))
                lsum = ea(nc.sbuf_tensor("lsum", [8, 1], F32))
                rrow = ea(nc.sbuf_tensor("rrow", [1, 8], F32))
                erow = ea(nc.sbuf_tensor("erow", [1, 8], F32))
                rmx = ea(nc.sbuf_tensor("rmx", [1, 1], F32))
                rsm = ea(nc.sbuf_tensor("rsm", [1, 1], F32))
                rrec = ea(nc.sbuf_tensor("rrec", [1, 1], F32))
                rmask_t = ea(nc.sbuf_tensor("rmask_t", [8, T], F32))

                slots = [pYd[:, 0:512], pYd[:, 512:1024], pYo[:, 0:512],
                         pYo[:, 512:1024], pF[:, 0:512], pF[:, 512:1024]]

                with nc.Block("passA") as blk:

                    @blk.sync
                    def _(sy):
                        bar_in(sy)
                        for dst, src in ((dtb_t, dtb), (alog_t, alog),
                                         (rb_t, rbv), (rmask_t, rmask),
                                         (lm_t, lmask), (ilm_t, ilmask),
                                         (bp_t, bpv)):
                            S["s_sml"].inc(sy.dma_start(out=dst[:], in_=src), 16)
                        S["s_sml"].inc(
                            sy.dma_start(out=dbc[:], in_=bc_ap(dvec, 128, 128)), 16)
                        off = 0
                        for tt, w in enumerate(TTS):
                            sy.wait_ge(S["s_lna"].h, tt + 1)
                            S["s_lnro"].inc(
                                sy.dma_start(out=d_rstd[:, off:off + w],
                                             in_=r_rstd[:, 0:w]), 16)
                            sy.wait_ge(S["s_ln2"].h, tt + 1)
                            S["s_lnr2"].inc(
                                sy.dma_start(out=d_mr[:, off:off + w],
                                             in_=r_mr[:, 0:w]), 16)
                            off += w
                        sy.wait_ge(S["s_cc"].h, 1)
                        S["s_aro"].inc(sy.dma_start(out=rrow[:], in_=ar_out), 16)
                        sy.wait_ge(S["s_rt3"].h, 2)
                        S["s_rt4"].inc(sy.dma_start(out=d_rt, in_=rrow[:]), 16)
                        sy.wait_ge(S["s_rt4"].h, 16)
                        S["s_rt5"].inc(
                            sy.dma_start(out=rr_col[:],
                                         in_=bass.AP(d_rt.tensor, 0,
                                                     [[1, 8], [0, 16], [0, 1]])),
                            16)
                        for i, cb in enumerate(staged):
                            sem, val = ev_sched[(CB_POS[cb] + 1) * NTT - 1]
                            sy.wait_ge(S[sem].h, val)
                            if cb < NB_GATE:
                                dst = d_gate[cb]
                            elif cb < 144:
                                dst = d_hbc[cb - NB_GATE]
                            else:
                                dst = d_ffnb[cb - 147]
                            S["s_st"].inc(sy.dma_start(out=dst, in_=stage[:]), 16)
                        sy.wait_ge(S["s_st"].h, 16 * len(staged))
                        sy.wait_ge(S["s_rt5"].h, 16)
                        bar_out(sy)

                    @blk.gpsimd
                    def _(g):
                        bar_in(g)
                        g.memset(ident_bf[:], 0.0)
                        g.affine_select(out=ident_bf[:], in_=ident_bf[:],
                                        compare_op=ALU.not_equal, fill=1.0,
                                        base=0, pattern=[[-1, 128]],
                                        channel_multiplier=1)
                        g.memset(ident_f[:], 0.0)
                        g.affine_select(out=ident_f[:], in_=ident_f[:],
                                        compare_op=ALU.not_equal, fill=1.0,
                                        base=0, pattern=[[-1, 128]],
                                        channel_multiplier=1)
                        g.memset(trimask[:], 1.0)
                        g.affine_select(out=trimask[:], in_=trimask[:],
                                        compare_op=ALU.is_ge, fill=0.0, base=0,
                                        pattern=[[1, 128]],
                                        channel_multiplier=-1)
                        g.memset(ones128[:], 1.0)
                        g.memset(onecol[:], 1.0)
                        g.memset(cdlog[0][:], 0.0)
                        S["s_setup"].inc(g.nop())
                        for kb in range(NKB):
                            S["s_x"].inc(
                                g.dma_start(out=xnreg[:, kb * T:(kb + 1) * T],
                                            in_=x_fm[kb * 128:(kb + 1) * 128, :]),
                                16)
                        S["s_tap"].inc(g.dma_start(out=cw_bf[:], in_=cw), 16)
                        S["s_tap"].inc(g.dma_start(out=cb_bf[:], in_=cbv), 16)
                        g.wait_ge(S["s_lnro"].h, 16 * NTT)
                        g.wait_ge(S["s_lnr2"].h, 16 * NTT)
                        S["s_lnbc"].inc(
                            g.dma_start(out=rbc_bf[:], in_=bc_ap(d_rstd, 128, T)),
                            16)
                        S["s_lnbc"].inc(
                            g.dma_start(out=mrb_bf[:], in_=bc_ap(d_mr, 128, T)),
                            16)
                        for idx, cb in enumerate(CB_ORDER):
                            if idx >= 2:
                                g.wait_ge(S["s_mm"].h, (idx - 1) * NTT)
                            wsem = S["s_w1"] if idx % 2 else S["s_w"]
                            wsem.inc(
                                g.dma_start(
                                    out=wbuf[idx % 2][:],
                                    in_=wa.rearrange("(kb p) c -> p kb c", p=128)
                                    [:, :, cb * 128:(cb + 1) * 128]), 16)
                            if idx == 3:
                                g.wait_ge(S["s_rt"].h, 1)
                                S["s_ari"].inc(
                                    g.dma_start(out=ar_in, in_=lsum[:]), 16)
                                g.wait_ge(S["s_ari"].h, 16)
                                S["s_cc"].inc(
                                    g.collective_compute(
                                        "AllReduce", ALU.add,
                                        replica_groups=[list(range(NCORE))],
                                        ins=[ar_in], outs=[ar_out]))
                        bar_out(g)

                    @blk.tensor
                    def _(t):
                        bar_in(t)
                        t.wait_ge(S["s_setup"].h, 1)
                        off = 0
                        for tt, w in enumerate(TTS):
                            if tt > 0:
                                t.wait_ge(S["s_stev"].h, tt)
                            for kb in range(NKB):
                                t.wait_ge(S["s_x"].h, 16 * NKB)
                                t.matmul(pAux[0:1, 0:w], onecol[:],
                                         xnreg[:, kb * T + off: kb * T + off + w],
                                         start=(kb == 0), stop=(kb == NKB - 1))
                                t.wait_ge(S["s_sq"].h, tt * NKB + kb + 1)
                                ins = t.matmul(pAux[0:1, 512:512 + w], onecol[:],
                                               x2t[kb % 2][:, 0:w],
                                               start=(kb == 0),
                                               stop=(kb == NKB - 1))
                                if kb < NKB - 1:
                                    S["s_smm"].inc(ins)
                            S["s_stats"].inc(ins)
                            off += w
                        t.wait_ge(S["s_xn"].h, NKB)
                        gi = 0
                        for idx, cb in enumerate(CB_ORDER):
                            t.wait_ge(S["s_w1" if idx % 2 else "s_w"].h,
                                      16 * (idx // 2 + 1))
                            off = 0
                            for tt, w in enumerate(TTS):
                                if gi >= 6:
                                    sem, val = ev_sched[gi - 6]
                                    t.wait_ge(S[sem].h, val)
                                ps = slots[gi % 6]
                                for kb in range(NKB):
                                    ins = t.matmul(
                                        ps[:, 0:w],
                                        wbuf[idx % 2][:, kb * 128:(kb + 1) * 128],
                                        xnreg[:, kb * T + off: kb * T + off + w],
                                        start=(kb == 0), stop=(kb == NKB - 1))
                                S["s_mm"].inc(ins)
                                gi += 1
                                off += w
                        bar_out(t)

                    @blk.scalar
                    def _(sc):
                        bar_in(sc)
                        def _lnrow(tt):
                            sc.wait_ge(S["s_ln1"].h, tt + 1)
                            if tt > 0:
                                sc.wait_ge(S["s_lnro"].h, 16 * tt)
                            sc.activation(r_s2[:, 0:TTM], r_t[:], AF.Ln)
                            S["s_lna"].inc(
                                sc.activation(r_rstd[:], r_s2[:, 0:TTM], AF.Exp,
                                              scale=-0.5))

                        off = 0
                        for tt, w in enumerate(TTS):
                            for kb in range(NKB):
                                sc.wait_ge(S["s_x"].h, 16 * NKB)
                                if kb >= 2:
                                    sc.wait_ge(S["s_smm"].h,
                                               tt * (NKB - 1) + kb - 1)
                                elif kb == 0 and tt > 0:
                                    sc.wait_ge(S["s_smm"].h, tt * (NKB - 1))
                                elif kb == 1 and tt > 0:
                                    sc.wait_ge(S["s_stats"].h, tt)
                                S["s_sq"].inc(
                                    sc.activation(
                                        x2t[kb % 2][:, 0:w],
                                        xnreg[:, kb * T + off: kb * T + off + w],
                                        AF.Square))
                            off += w
                            if tt >= 1:
                                _lnrow(tt - 1)
                        _lnrow(NTT - 1)
                        gi = 0
                        for idx, cb in enumerate(CB_ORDER):
                            if idx == 4:
                                sc.wait_ge(S["s_rt2"].h, 1)
                                sc.activation(erow[:], rrow[:], AF.Exp)
                                S["s_rt3"].inc(sc.copy(rrow[:], erow[:]))
                            off = 0
                            for tt, w in enumerate(TTS):
                                if 0 <= cb < NB_GATE:
                                    sc.wait_ge(S["s_mm"].h, gi + 1)
                                    si = st_idx[cb]
                                    if tt == 0 and si >= 1:
                                        sc.wait_ge(S["s_st"].h, 16 * si)
                                    S["s_eva"].inc(
                                        sc.activation(stage[:, off:off + w],
                                                      slots[gi % 6][:, 0:w],
                                                      AF.Silu))
                                gi += 1
                                off += w
                        bar_out(sc)

                    @blk.vector
                    def _(v):
                        bar_in(v)
                        v.wait_ge(S["s_sml"].h, 16 * 8)
                        for tt, w in enumerate(TTS):
                            v.wait_ge(S["s_stats"].h, tt + 1)
                            v.tensor_copy(r_s1[:, 0:w], pAux[0:1, 0:w])
                            S["s_stev"].inc(
                                v.tensor_copy(r_s2[:, 0:w],
                                              pAux[0:1, 512:512 + w]))
                            v.tensor_scalar(r_m[:, 0:w], r_s1[:, 0:w],
                                            1.0 / HID, None, ALU.mult)
                            v.tensor_tensor(r_t[:, 0:w], r_m[:, 0:w],
                                            r_m[:, 0:w], ALU.mult)
                            v.scalar_tensor_tensor(r_s1[:, 0:w], r_s2[:, 0:w],
                                                   1.0 / HID, r_t[:, 0:w],
                                                   ALU.mult, ALU.subtract)
                            v.tensor_scalar(r_t[:, 0:w], r_s1[:, 0:w],
                                            LN_EPS, None, ALU.add)
                            S["s_ln1"].inc(
                                v.tensor_copy(r_s1[:, 0:w], r_t[:, 0:w]))
                            v.wait_ge(S["s_lna"].h, tt + 1)
                            if tt > 0:
                                v.wait_ge(S["s_lnr2"].h, 16 * tt)
                            S["s_ln2"].inc(
                                v.tensor_tensor(r_mr[:, 0:w], r_m[:, 0:w],
                                                r_rstd[:, 0:w], ALU.mult))
                        v.wait_ge(S["s_sq"].h, NTT * NKB)
                        v.wait_ge(S["s_stats"].h, NTT)
                        v.wait_ge(S["s_lnbc"].h, 32)
                        for kb in range(NKB):
                            v.tensor_tensor(nstg[:], xnreg[:, kb * T:(kb + 1) * T],
                                            rbc_bf[:], ALU.mult)
                            S["s_xn"].inc(
                                v.tensor_tensor(xnreg[:, kb * T:(kb + 1) * T],
                                                nstg[:], mrb_bf[:],
                                                ALU.subtract))
                        gi = 0
                        for idx, cb in enumerate(CB_ORDER):
                            off = 0
                            for tt, w in enumerate(TTS):
                                ps = slots[gi % 6]
                                if not (0 <= cb < NB_GATE):
                                    v.wait_ge(S["s_mm"].h, gi + 1)
                                    if cb == CB_DT:
                                        ins = v.tensor_copy(
                                            dt_raw[:, off:off + w], ps[:, 0:w])
                                    elif cb == CB_LORA:
                                        ins = v.tensor_copy(
                                            t_sc[:, off:off + w], ps[:, 0:w])
                                    elif cb == CB_RT:
                                        ins = v.tensor_tensor(
                                            tmp8[:, 0:w], ps[0:8, 0:w],
                                            rmask_t[:, off:off + w], ALU.mult)
                                        v.reduce_sum(red8[:], tmp8[:, 0:w],
                                                     axis=AX.X)
                                        v.tensor_copy(tmp8[:, 0:w],
                                                      rmask_t[:, off:off + w])
                                        v.tensor_copy(tmp8[:, 0:w],
                                                      rmask_t[:, off:off + w])
                                        if tt == 0:
                                            v.tensor_copy(lsum[:], red8[:])
                                        else:
                                            ins2 = v.tensor_tensor(
                                                lsum[:], lsum[:], red8[:],
                                                ALU.add)
                                            if tt == NTT - 1:
                                                S["s_rt"].inc(ins2)
                                    else:
                                        si = st_idx[cb]
                                        if tt == 0 and si >= 1:
                                            v.wait_ge(S["s_st"].h, 16 * si)
                                        if cb >= 147:
                                            ins = v.tensor_scalar(
                                                stage[:, off:off + w],
                                                ps[:, 0:w],
                                                bp_t[:, cb - 147:cb - 146],
                                                None, ALU.add)
                                        else:
                                            ins = v.tensor_copy(
                                                stage[:, off:off + w],
                                                ps[:, 0:w])
                                    S["s_evv"].inc(ins)
                                gi += 1
                                off += w
                            if idx == 8:
                                v.wait_ge(S["s_aro"].h, 16)
                                v.tensor_tensor(erow[:], rrow[:], rb_t[:],
                                                ALU.add)
                                v.reduce_max(rmx[:], erow[:], axis=AX.X)
                                v.tensor_copy(tmp8[:], rmask_t[:, 0:TTM])
                                v.tensor_copy(tmp8[:], rmask_t[:, 0:TTM])
                                v.tensor_scalar(rrow[:], erow[:], rmx[:],
                                                None, ALU.subtract)
                                S["s_rt2"].inc(v.tensor_copy(erow[:], rrow[:]))
                            if idx == 12:
                                v.wait_ge(S["s_rt3"].h, 1)
                                v.reduce_sum(rsm[:], erow[:], axis=AX.X)
                                v.tensor_copy(tmp8[:], rmask_t[:, 0:TTM])
                                v.tensor_copy(tmp8[:], rmask_t[:, 0:TTM])
                                v.reciprocal(rrec[:], rsm[:])
                                v.tensor_copy(tmp8[:], rmask_t[:, 0:TTM])
                                v.tensor_copy(tmp8[:], rmask_t[:, 0:TTM])
                                v.tensor_copy(tmp8[:], rmask_t[:, 0:TTM])
                                S["s_rt3"].inc(
                                    v.tensor_scalar(rrow[:], erow[:], rrec[:],
                                                    LORA_SCALE, ALU.mult,
                                                    ALU.mult))
                        bar_out(v)

            phase[0] += 1

        pass_a()
        if STOP_AFTER != "A":
            _build_bgcd(nc, locals())
    return nc


def _build_bgcd(nc, env):
    S = env["S"]
    bar_in, bar_out = env["bar_in"], env["bar_out"]
    phase = env["phase"]
    bc_ap = env["bc_ap"]
    T, tok, NCH, taps = env["T"], env["tok"], env["NCH"], env["taps"]
    TSW, NTS, NSUB = env["TSW"], env["NTS"], env["NSUB"]
    pYd, pYo, pF, pAux = env["pYd"], env["pYo"], env["pF"], env["pAux"]
    dt_raw, t_sc, cdlog = env["dt_raw"], env["t_sc"], env["cdlog"]
    ident_bf, ident_f = env["ident_bf"], env["ident_f"]
    trimask, ones128 = env["trimask"], env["ones128"]
    cw_bf, cb_bf = env["cw_bf"], env["cb_bf"]
    dtb_t, acol, alog_t, dbc = env["dtb_t"], env["acol"], env["alog_t"], env["dbc"]
    lm_t, ilm_t, rr_col = env["lm_t"], env["ilm_t"], env["rr_col"]
    d_gate, d_hbc, d_ffnb = env["d_gate"], env["d_hbc"], env["d_ffnb"]
    d_yloc, d_norm, d_acs = env["d_yloc"], env["d_norm"], env["d_acs"]
    d_cd, d_dtot, d_dst = env["d_cd"], env["d_dtot"], env["d_dst"]
    cc_in, cc_out = env["cc_in"], env["cc_out"]
    x_fm, wd, nwv, out = env["x_fm"], env["wd"], env["nwv"], env["out"]
    d_sbf, d_ebf = env["d_sbf"], env["d_ebf"]
    d_cco = env["d_cco"]

    pAux_bf = pAux[:].bitcast(BF16)

    with ExitStack() as sBC:
        eb = sBC.enter_context
        cx_all = eb(nc.sbuf_tensor("cx_all", [128, NCH * 1024], BF16))
        sdo_all = eb(nc.sbuf_tensor("sdo_all", [128, NCH * 128], F32))
        s_bf = eb(nc.sbuf_tensor("s_bf", [128, INTER], BF16))

        with ExitStack() as sB:
            e2 = sB.enter_context
            hbuf = e2(nc.sbuf_tensor("hbuf", [128, 20 * 131], BF16))
            qa = e2(nc.sbuf_tensor("qa", [128, 2560], BF16))
            qb = e2(nc.sbuf_tensor("qb", [128, 2560], BF16))
            qc = e2(nc.sbuf_tensor("qc", [128, 2560], BF16))
            qd = e2(nc.sbuf_tensor("qd", [128, 2560], BF16))
            cs2 = e2(nc.sbuf_tensor("cs2", [128, 72 * 128], BF16))
            sdt = e2(nc.sbuf_tensor("sdt", [128, 128], F32))
            mdt = e2(nc.sbuf_tensor("mdt", [128, 128], F32))
            adt = e2(nc.sbuf_tensor("adt", [128, 128], F32))
            edt = e2(nc.sbuf_tensor("edt", [128, 128], F32))
            ldt = e2(nc.sbuf_tensor("ldt", [128, 128], F32))
            dtc = e2(nc.sbuf_tensor("dtc", [128, 128], F32))
            admt = e2(nc.sbuf_tensor("admt", [128, 128], F32))
            a_cs = e2(nc.sbuf_tensor("a_cs", [128, 128], F32))
            a_csT = e2(nc.sbuf_tensor("a_csT", [128, 128], F32))
            dtT_f = e2(nc.sbuf_tensor("dtT_f", [128, 128], F32))
            dtT_bf = e2(nc.sbuf_tensor("dtT_bf", [128, 128], BF16))
            decT_bf = e2(nc.sbuf_tensor("decT_bf", [128, 128], BF16))
            ddtT_bf = e2(nc.sbuf_tensor("ddtT_bf", [128, 128], BF16))
            rdt = e2(nc.sbuf_tensor("rdt", [128, 128], F32))
            dec_fm = e2(nc.sbuf_tensor("dec_fm", [128, 128], F32))
            sdoT_raw = e2(nc.sbuf_tensor("sdoT_raw", [128, 128], F32))
            cdb = e2(nc.sbuf_tensor("cdb", [128, 128], F32))
            dtotb = e2(nc.sbuf_tensor("dtotb", [128, 128], BF16))
            cdxp = e2(nc.sbuf_tensor("cdxp", [128, 1], F32))
            dttc = e2(nc.sbuf_tensor("dttc", [128, 1], F32))
            x_tm = e2(nc.sbuf_tensor("x_tm", [128, INTER], BF16))
            bx_tm = e2(nc.sbuf_tensor("bx_tm", [128, 1024], BF16))
            rbcb = e2(nc.sbuf_tensor("rbcb", [128, 2048], F32))
            dmt = e2(nc.sbuf_tensor("dmt", [128, 2048], F32))
            dmc = e2(nc.sbuf_tensor("dmc", [128, 2048], F32))
            ebuf = e2(nc.sbuf_tensor("ebuf", [128, 2048], BF16))
            mtt = [e2(nc.sbuf_tensor(f"mtt{i}", [128, 2048], BF16))
                   for i in range(2)]
            bdt = [e2(nc.sbuf_tensor(f"bdt{i}", [128, 2048], BF16))
                   for i in range(2)]
            gmm = e2(nc.sbuf_tensor("gmm", [128, 1024], BF16))
            tev1 = e2(nc.sbuf_tensor("tev1", [128, 1024], F32))
            tev2 = e2(nc.sbuf_tensor("tev2", [128, 1024], BF16))
            dres = e2(nc.sbuf_tensor("dres", [128, 1024], BF16))
            ystage = [e2(nc.sbuf_tensor(f"ystage{i}", [128, 1024], F32))
                      for i in range(2)]
            dumpV = e2(nc.sbuf_tensor("dumpV", [128, 128], F32))
            dumpA = e2(nc.sbuf_tensor("dumpA", [128, 128], F32))

            hview = hbuf[:].rearrange("p (b t) -> p b t", b=20)
            cwv = cw_bf[:].rearrange("p (b k) -> p b k", k=KW)

            with nc.Block("passB") as blk:

                @blk.sync
                def _(sy):
                    bar_in(sy)
                    for c in range(NCH):
                        base = 128 * c
                        for q in range(4):
                            sy.wait_ge(S["s_conv"].h, 4 * c + q)
                            S["s_hbcl"].inc(
                                sy.dma_start(
                                    out=hview,
                                    in_=d_hbc[20 * q:20 * q + 20, :,
                                              base:base + 131]
                                    .rearrange("b p t -> p b t")), 16)
                        sy.wait_ge(S["s_acs"].h, c + 1)
                        S["s_acso"].inc(sy.dma_start(out=d_acs[c], in_=a_cs[:]), 16)
                        sy.wait_ge(S["s_cde"].h, c + 1)
                        S["s_sm1"].inc(sy.dma_start(out=d_cd, in_=cdxp[:]), 16)
                        sy.wait_ge(S["s_sm1"].h, 16 * (c + 1))
                        if c > 0:
                            sy.wait_ge(S["s_sdo"].h, c)
                        S["s_sm2"].inc(
                            sy.dma_start(out=cdb[:], in_=bc_ap(d_cd, 128, 128)), 16)
                        sy.wait_ge(S["s_dtt"].h, c + 1)
                        S["s_sm3"].inc(sy.dma_start(out=d_dtot, in_=dttc[:]), 16)
                        for hb in range(8):
                            sy.wait_ge(S["s_acso"].h, 16 * (c + 1))
                            if 8 * c + hb >= 1:
                                sy.wait_ge(S["s_mtd"].h, 8 * c + hb)
                            S["s_rbc"].inc(
                                sy.dma_start(
                                    out=rbcb[:]
                                    .rearrange("p (h l) -> p h l", h=16),
                                    in_=bass.AP(d_acs.tensor,
                                                (c * 128 + 16 * hb) * 128,
                                                [[0, 128], [128, 16], [1, 128]])),
                                16)
                            if hb >= 1:
                                sy.wait_ge(S["s_yev"].h, 8 * c + hb)
                                S["s_yst"].inc(
                                    sy.dma_start(
                                        out=d_yloc[c][:, 1024 * (hb - 1):
                                                      1024 * hb],
                                        in_=ystage[(hb - 1) % 2][:]), 16)
                        sy.wait_ge(S["s_yev"].h, 8 * c + 8)
                        S["s_yst"].inc(
                            sy.dma_start(out=d_yloc[c][:, 7168:8192],
                                         in_=ystage[7 % 2][:]), 16)
                    sy.wait_ge(S["s_yst"].h, 16 * 8 * NCH)
                    sy.wait_ge(S["s_acso"].h, 16 * NCH)
                    sy.wait_ge(S["s_sm1"].h, 16 * NCH)
                    sy.wait_ge(S["s_sm3"].h, 16 * NCH)
                    bar_out(sy)

                @blk.gpsimd
                def _(g):
                    bar_in(g)
                    g.memset(s_bf[:], 0.0)
                    for c in range(NCH):
                        g.wait_ge(S["s_sm3"].h, 16 * (c + 1))
                        g.wait_ge(S["s_sup"].h, 8 * c)
                        S["s_sm4"].inc(
                            g.dma_start(out=dtotb[:], in_=bc_ap(d_dtot, 128, 128)),
                            16)
                    bar_out(g)

                @blk.scalar
                def _(sc):
                    bar_in(sc)
                    S["s_dec"].inc(sc.activation(dec_fm[:, 0:1], alog_t[:], AF.Exp))
                    for c in range(NCH):
                        sc.wait_ge(S["s_dt1"].h, c + 1)
                        sc.activation(adt[:], sdt[:], AF.Abs)
                        sc.activation(edt[:], adt[:], AF.Exp, scale=-1.0)
                        S["s_dta"].inc(sc.activation(ldt[:], edt[:], AF.Ln,
                                                     bias=1.0))
                        sc.wait_ge(S["s_acs"].h, c + 1)
                        sc.activation(dec_fm[:], a_cs[:], AF.Exp, scale=-1.0,
                                      bias=a_cs[:, 127:128])
                        S["s_dec"].inc(sc.copy(dumpA[:], dec_fm[:]))
                        sc.wait_ge(S["s_cdu"].h, c)
                        if c > 0:
                            sc.wait_ge(S["s_sm1"].h, 16 * c)
                        S["s_cde"].inc(
                            sc.activation(cdxp[:], cdlog[c % 2][:], AF.Exp))
                        if c > 0:
                            sc.wait_ge(S["s_sm3"].h, 16 * c)
                        S["s_dtt"].inc(
                            sc.activation(dttc[:], a_cs[:, 127:128], AF.Exp))
                        for q in range(4):
                            sc.wait_ge(S["s_conv"].h, 4 * c + q + 1)
                            if q < 3:
                                S["s_silu"].inc(
                                    sc.activation(cs2[:, q * 2560:(q + 1) * 2560],
                                                  qb[:], AF.Silu))
                            else:
                                sc.activation(cs2[:, 7680:9216], qb[:, 0:1536],
                                              AF.Silu)
                                S["s_silu"].inc(
                                    sc.activation(
                                        cx_all[:, c * 1024:(c + 1) * 1024],
                                        qb[:, 1536:2560], AF.Silu))
                        sc.wait_ge(S["s_trevA"].h, c + 1)
                        sc.activation(sdoT_raw[:], a_csT[:], AF.Exp)
                        S["s_sdr"].inc(sc.copy(dumpA[:], sdoT_raw[:]))
                        for hb in range(8):
                            sc.wait_ge(S["s_mtd"].h, 8 * c + hb + 1)
                            sc.wait_ge(S["s_mt"].h, 8 * c + hb)
                            S["s_mte"].inc(
                                sc.activation(ebuf[:], dmc[:], AF.Exp))
                    bar_out(sc)

                @blk.tensor
                def _(t):
                    bar_in(t)
                    for c in range(NCH):
                        t.wait_ge(S["s_acs"].h, c + 1)
                        t.wait_ge(S["s_dec"].h, c + 2)
                        t.wait_ge(S["s_gmm"].h, c)
                        t.wait_ge(S["s_trevA"].h, c)
                        t.transpose(pAux[:, 0:128], a_cs[:], ident_f[:])
                        t.transpose(pAux[:, 128:256], dtc[:], ident_f[:])
                        ins = t.transpose(pAux[:, 256:384], dec_fm[:], ident_f[:])
                        S["s_trnA"].inc(ins)
                        for gi in range(9):
                            qreq = min(4, (8 * gi + 7) // 20 + 1)
                            t.wait_ge(S["s_silu"].h, 4 * c + qreq)
                            if gi == 0:
                                t.wait_ge(S["s_trevA"].h, c + 1)
                            if 9 * c + gi >= 2:
                                t.wait_ge(S["s_trevH"].h, 9 * c + gi - 1)
                            half = pAux_bf[:, 1024 * (gi % 2):1024 * (gi % 2) + 1024]
                            for j in range(8):
                                ins = t.transpose(
                                    half[:, j * 128:(j + 1) * 128],
                                    cs2[:, (8 * gi + j) * 128:(8 * gi + j + 1) * 128],
                                    ident_bf[:])
                            S["s_trnH"].inc(ins)
                        t.wait_ge(S["s_trevH"].h, 9 * c + 9)
                        t.wait_ge(S["s_silu"].h, 4 * c + 4)
                        for g8 in range(8):
                            ins = t.matmul(
                                pAux[:, g8 * 128:(g8 + 1) * 128],
                                cs2[:, (64 + g8) * 128:(65 + g8) * 128],
                                cx_all[:, c * 1024 + g8 * 128:
                                       c * 1024 + (g8 + 1) * 128],
                                start=True, stop=True)
                        S["s_gmt"].inc(ins)
                        for hb in range(8):
                            t.wait_ge(S["s_mt"].h, 8 * c + hb + 1)
                            t.wait_ge(S["s_yev"].h, 8 * c + hb)
                            t.wait_ge(S["s_trevH"].h, 9 * c + hb + 1)
                            for i in range(16):
                                h = 16 * hb + i
                                ins = t.matmul(
                                    pYd[:, i * 64:(i + 1) * 64],
                                    mtt[hb % 2][:, i * 128:(i + 1) * 128],
                                    x_tm[:, h * 64:(h + 1) * 64],
                                    start=True, stop=True)
                            S["s_yd"].inc(ins)
                            if c > 0:
                                t.wait_ge(S["s_sup"].h, 8 * (c - 1) + hb + 1)
                            for i in range(16):
                                h = 16 * hb + i
                                ins = t.matmul(
                                    pYo[:, i * 64:(i + 1) * 64],
                                    cx_all[:, c * 1024 + (i % 8) * 128:
                                           c * 1024 + (i % 8 + 1) * 128],
                                    s_bf[:, h * 64:(h + 1) * 64],
                                    start=True, stop=True)
                            S["s_yo"].inc(ins)
                            t.wait_ge(S["s_bd"].h, 8 * c + hb + 1)
                            t.wait_ge(S["s_sup"].h, 8 * c + hb)
                            for i in range(16):
                                h = 16 * hb + i
                                ins = t.matmul(
                                    pF[:, i * 64:(i + 1) * 64],
                                    bdt[hb % 2][:, i * 128:(i + 1) * 128],
                                    x_tm[:, h * 64:(h + 1) * 64],
                                    start=True, stop=True)
                            S["s_f"].inc(ins)
                    bar_out(t)

                @blk.vector
                def _(v):
                    bar_in(v)
                    v.wait_ge(S["s_dec"].h, 1)
                    v.wait_ge(S["s_tap"].h, 32)
                    v.tensor_scalar(acol[:], dec_fm[:, 0:1], -1.0, None, ALU.mult)

                    def conv_q(c, q):
                        v.wait_ge(S["s_hbcl"].h, 16 * (4 * c + q + 1))
                        wv = [cwv[:, 20 * q:20 * q + 20, k].unsqueeze(2)
                              .to_broadcast((128, 20, 128)) for k in range(KW)]
                        bv = (cb_bf[:].rearrange("p b -> p b")[:, 20 * q:20 * q + 20]
                              .unsqueeze(2).to_broadcast((128, 20, 128)))
                        qa3 = qa[:].rearrange("p (b t) -> p b t", b=20)
                        qb3 = qb[:].rearrange("p (b t) -> p b t", b=20)
                        qc3 = qc[:].rearrange("p (b t) -> p b t", b=20)
                        qd3 = qd[:].rearrange("p (b t) -> p b t", b=20)
                        v.tensor_tensor(qa3, hview[:, :, 0:128], wv[0], ALU.mult)
                        if 4 * c + q >= 1:
                            v.wait_ge(S["s_silu"].h, 4 * c + q)
                        v.tensor_tensor(qb3, hview[:, :, 1:129], wv[1], ALU.mult)
                        v.tensor_tensor(qc3, qa3, qb3, ALU.add)
                        v.tensor_tensor(qa3, hview[:, :, 2:130], wv[2], ALU.mult)
                        v.tensor_tensor(qb3, hview[:, :, 3:131], wv[3], ALU.mult)
                        v.tensor_tensor(qd3, qa3, qb3, ALU.add)
                        v.tensor_tensor(qa3, qc3, qd3, ALU.add)
                        S["s_conv"].inc(v.tensor_tensor(qb3, qa3, bv, ALU.add))

                    def tail(c, b):
                        v.wait_ge(S["s_mte"].h, 8 * c + b + 1)
                        if 8 * c + b >= 2:
                            v.wait_ge(S["s_yd"].h, 8 * c + b - 1)
                        ev = ebuf[:].rearrange("s (r g l) -> s r g l", r=2, g=8)
                        gv = (gmm[:].rearrange("s (g l) -> s g l", g=8)
                              .unsqueeze(1).to_broadcast((128, 2, 8, 128)))
                        S["s_mt"].inc(
                            v.tensor_tensor(
                                mtt[b % 2][:].rearrange("s (r g l) -> s r g l",
                                                        r=2, g=8),
                                ev, gv, ALU.mult))
                        if 8 * c + b >= 2:
                            v.wait_ge(S["s_f"].h, 8 * c + b - 1)
                        bxv = (bx_tm[:].rearrange("s (g n) -> s g n", g=8)
                               .unsqueeze(1).to_broadcast((128, 2, 8, 128)))
                        dcv = (decT_bf[:, 16 * b:16 * b + 16]
                               .rearrange("s (r g) -> s r g", r=2)
                               .unsqueeze(3).to_broadcast((128, 2, 8, 128)))
                        S["s_bd"].inc(
                            v.tensor_tensor(
                                bdt[b % 2][:].rearrange("s (r g n) -> s r g n",
                                                        r=2, g=8),
                                bxv, dcv, ALU.mult))
                        v.wait_ge(S["s_yd"].h, 8 * c + b + 1)
                        v.wait_ge(S["s_yo"].h, 8 * c + b + 1)
                        if 8 * c + b >= 2:
                            v.wait_ge(S["s_yst"].h, 16 * (8 * c + b - 1))
                        sv = (sdo_all[:, c * 128 + 16 * b:c * 128 + 16 * b + 16]
                              .unsqueeze(2).to_broadcast((128, 16, 64)))
                        v.tensor_tensor(
                            tev1[:].rearrange("l (h p) -> l h p", h=16),
                            pYo[:].rearrange("l (h p) -> l h p", h=16),
                            sv, ALU.mult)
                        v.tensor_tensor(tev2[:], tev1[:], pYd[:], ALU.add)
                        dv = (ddtT_bf[:, 16 * b:16 * b + 16]
                              .unsqueeze(2).to_broadcast((128, 16, 64)))
                        v.tensor_tensor(
                            dres[:].rearrange("s (h p) -> s h p", h=16),
                            x_tm[:, 1024 * b:1024 * (b + 1)]
                            .rearrange("s (h p) -> s h p", h=16),
                            dv, ALU.mult)
                        S["s_yev"].inc(
                            v.tensor_tensor(ystage[b % 2][:], tev2[:], dres[:],
                                            ALU.add))
                        v.wait_ge(S["s_f"].h, 8 * c + b + 1)
                        v.wait_ge(S["s_sm4"].h, 16 * (c + 1))
                        dtv = (dtotb[:, 16 * b:16 * b + 16]
                               .unsqueeze(2).to_broadcast((128, 16, 64)))
                        v.tensor_tensor(
                            tev1[:].rearrange("n (h p) -> n h p", h=16),
                            s_bf[:, 1024 * b:1024 * (b + 1)]
                            .rearrange("n (h p) -> n h p", h=16),
                            dtv, ALU.mult)
                        S["s_sup"].inc(
                            v.tensor_tensor(s_bf[:, 1024 * b:1024 * (b + 1)],
                                            tev1[:], pF[:], ALU.add))

                    for c in range(NCH):
                        T0 = 3 + 128 * c
                        v.tensor_scalar(sdt[:], dt_raw[:, T0:T0 + 128], dtb_t[:],
                                        None, ALU.add)
                        S["s_dt1"].inc(v.tensor_copy(dumpV[:], sdt[:]))
                        conv_q(c, 0)
                        v.wait_ge(S["s_dta"].h, c + 1)
                        v.tensor_scalar(mdt[:], sdt[:], 0.0, None, ALU.max)
                        v.tensor_tensor(dtc[:], mdt[:], ldt[:], ALU.add)
                        v.tensor_scalar(admt[:], dtc[:], acol[:], None, ALU.mult)
                        if c > 0:
                            v.wait_ge(S["s_acso"].h, 16 * c)
                        v.tensor_tensor_scan(a_cs[:], ones128[:], admt[:],
                                             0.0, ALU.mult, ALU.add)
                        v.tensor_copy(dumpV[:], a_cs[:])
                        S["s_acs"].inc(v.tensor_copy(dumpV[:], dtc[:]))
                        v.wait_ge(S["s_cde"].h, c)
                        S["s_cdu"].inc(
                            v.tensor_tensor(cdlog[(c + 1) % 2][:], cdlog[c % 2][:],
                                            a_cs[:, 127:128], ALU.add))
                        conv_q(c, 1)
                        v.wait_ge(S["s_trnA"].h, c + 1)
                        v.tensor_copy(a_csT[:], pAux[:, 0:128])
                        v.tensor_copy(dtT_f[:], pAux[:, 128:256])
                        v.tensor_copy(dtT_bf[:], pAux[:, 128:256])
                        v.tensor_copy(decT_bf[:], pAux[:, 256:384])
                        v.reciprocal(rdt[:], dtT_f[:])
                        v.tensor_copy(dumpV[:], trimask[:])
                        v.tensor_copy(dumpV[:], ones128[:])
                        v.tensor_tensor(ddtT_bf[:], rdt[:], dbc[:], ALU.mult)
                        S["s_trevA"].inc(v.tensor_copy(dumpV[:], a_csT[:]))
                        conv_q(c, 2)
                        conv_q(c, 3)
                        v.wait_ge(S["s_sdr"].h, c + 1)
                        v.wait_ge(S["s_sm2"].h, 16 * (c + 1))
                        S["s_sdo"].inc(
                            v.tensor_tensor(sdo_all[:, c * 128:(c + 1) * 128],
                                            sdoT_raw[:], cdb[:], ALU.mult))
                        for gi in range(9):
                            v.wait_ge(S["s_trnH"].h, 9 * c + gi + 1)
                            half = pAux_bf[:, 1024 * (gi % 2):
                                           1024 * (gi % 2) + 1024]
                            if gi < 8:
                                dtv2 = (dtT_bf[:, 16 * gi:16 * gi + 16]
                                        .rearrange("s (j h2) -> s j h2", j=8)
                                        .unsqueeze(3)
                                        .to_broadcast((128, 8, 2, 64)))
                                S["s_trevH"].inc(
                                    v.tensor_tensor(
                                        x_tm[:, gi * 1024:(gi + 1) * 1024]
                                        .rearrange("s (j h2 p) -> s j h2 p",
                                                   j=8, h2=2),
                                        half[:].rearrange(
                                            "s (j h2 p) -> s j h2 p",
                                            j=8, h2=2),
                                        dtv2, ALU.mult))
                            else:
                                S["s_trevH"].inc(
                                    v.tensor_copy(bx_tm[:], half[:]))
                        v.wait_ge(S["s_gmt"].h, c + 1)
                        tv = (trimask[:].unsqueeze(1)
                              .to_broadcast((128, 8, 128)))
                        S["s_gmm"].inc(
                            v.tensor_tensor(
                                gmm[:].rearrange("s (g l) -> s g l", g=8),
                                pAux[:].rearrange("s (g l) -> s g l", g=8),
                                tv, ALU.mult))
                        for hb in range(8):
                            v.wait_ge(S["s_rbc"].h, 16 * (8 * c + hb + 1))
                            v.wait_ge(S["s_mte"].h, 8 * c + hb)
                            av = (a_csT[:, 16 * hb:16 * hb + 16]
                                  .unsqueeze(2).to_broadcast((128, 16, 128)))
                            v.tensor_tensor(
                                dmt[:].rearrange("s (h l) -> s h l", h=16),
                                rbcb[:].rearrange("s (h l) -> s h l", h=16),
                                av, ALU.subtract)
                            S["s_mtd"].inc(
                                v.tensor_scalar(dmc[:], dmt[:], 0.0, None,
                                                ALU.min))
                            if hb > 0:
                                tail(c, hb - 1)
                        tail(c, 7)
                    bar_out(v)

        phase[0] += 1

        if STOP_AFTER == "B":
            return
        with ExitStack() as sG:
            e3 = sG.enter_context
            Et = [e3(nc.sbuf_tensor(f"Et{i}", [128, INTER], F32)) for i in range(2)]
            Gt = [e3(nc.sbuf_tensor(f"Gt{i}", [128, INTER], F32)) for i in range(2)]
            gdl = [e3(nc.sbuf_tensor(f"gdl{i}", [128, 1], F32)) for i in range(2)]
            dsf = e3(nc.sbuf_tensor("dsf", [128, 1], F32))
            dlb = [e3(nc.sbuf_tensor(f"dlb{i}", [128, 128], F32)) for i in range(2)]
            dxc = e3(nc.sbuf_tensor("dxc", [128, 1], F32))
            dxb = e3(nc.sbuf_tensor("dxb", [128, 1], BF16))
            E_bf = s_bf

            with nc.Block("passG") as blk:

                @blk.scalar
                def _(sc):
                    bar_in(sc)
                    sc.activation(dxc[:], cdlog[NCH % 2][:], AF.Exp)
                    S["s_dstg"].inc(sc.copy(gdl[0][:], dxc[:]))
                    bar_out(sc)

                @blk.vector
                def _(v):
                    bar_in(v)
                    v.wait_ge(S["s_dstg"].h, 1)
                    S["s_dstg"].inc(v.tensor_copy(dxb[:], dxc[:]))
                    v.memset(Et[0][:], 0.0)
                    for j in range(NCORE - 1):
                        v.wait_ge(S["s_gt1" if j % 2 else "s_gt"].h,
                                  32 * (j // 2 + 1))
                        S["s_dstg"].inc(
                            v.tensor_scalar(dsf[:], gdl[j % 2][:],
                                            lm_t[:, j:j + 1],
                                            ilm_t[:, j:j + 1],
                                            ALU.mult, ALU.add))
                        v.wait_ge(S["s_dbc"].h, 32 * (j + 1))
                        dl = (dlb[j % 2][:].unsqueeze(2)
                              .to_broadcast((128, 128, 64)))
                        v.tensor_tensor(
                            Et[1][:].rearrange("n (h p) -> n h p", h=128),
                            Et[0][:].rearrange("n (h p) -> n h p", h=128),
                            dl, ALU.mult)
                        S["s_ecomb"].inc(
                            v.scalar_tensor_tensor(Et[0][:], Gt[j % 2][:],
                                                   lm_t[:, j:j + 1], Et[1][:],
                                                   ALU.mult, ALU.add))
                    v.wait_ge(S["s_blob"].h, 32)
                    if d_sbf is not None:
                        v.wait_ge(S["s_tap"].h, 48)
                    v.tensor_copy(E_bf[:], Et[0][:])
                    bar_out(v)

                @blk.sync
                def _(sy):
                    bar_in(sy)
                    sy.wait_ge(S["s_dstg"].h, 2)
                    S["s_blob"].inc(
                        sy.dma_start(out=cc_in[:, 0:INTER], in_=s_bf[:]), 16)
                    S["s_blob"].inc(
                        sy.dma_start(out=cc_in[:, INTER:INTER + 1], in_=dxb[:],
                                     allow_slow_non_contiguous=True), 16)
                    for j in range(NCORE - 1):
                        sy.wait_ge(S["s_dstg"].h, j + 3)
                        S["s_dbc"].inc(sy.dma_start(out=d_dst, in_=dsf[:]), 16)
                        if j >= 2:
                            sy.wait_ge(S["s_ecomb"].h, j - 1)
                        S["s_dbc"].inc(
                            sy.dma_start(out=dlb[j % 2][:],
                                         in_=bc_ap(d_dst, 128, 128)), 16)
                    sy.wait_ge(S["s_dbc"].h, 32 * (NCORE - 1))
                    if d_cco is not None:
                        sy.wait_ge(S["s_cc2"].h, 1)
                        S["s_tap"].inc(sy.dma_start(out=d_cco, in_=cc_out), 16)
                    if d_sbf is not None:
                        S["s_tap"].inc(sy.dma_start(out=d_sbf, in_=s_bf[:]), 16)
                    if d_ebf is not None:
                        sy.wait_ge(S["s_ecomb"].h, NCORE - 1)
                        S["s_tap"].inc(sy.dma_start(out=d_ebf, in_=E_bf[:]), 16)
                    bar_out(sy)

                @blk.gpsimd
                def _(g):
                    bar_in(g)
                    g.wait_ge(S["s_blob"].h, 32)
                    S["s_cc2"].inc(
                        g.collective_compute(
                            "AllGather", ALU.bypass,
                            replica_groups=[list(range(NCORE))],
                            ins=[cc_in], outs=[cc_out]))
                    g.wait_ge(S["s_cc2"].h, 1)
                    for j in range(NCORE - 1):
                        if j >= 2:
                            g.wait_ge(S["s_ecomb"].h, j - 1)
                        gsem = S["s_gt1"] if j % 2 else S["s_gt"]
                        gsem.inc(
                            g.dma_start(out=Gt[j % 2][:],
                                        in_=cc_out[j * 128:(j + 1) * 128,
                                                   0:INTER]), 16)
                        gsem.inc(
                            g.dma_start(out=gdl[j % 2][:],
                                        in_=cc_out[j * 128:(j + 1) * 128,
                                                   INTER:INTER + 1],
                                        allow_slow_non_contiguous=True), 16)
                    bar_out(g)

                @blk.tensor
                def _(t):
                    bar_in(t)
                    bar_out(t)

        phase[0] += 1
        env["E_bf"] = E_bf
        env["cx_all"] = cx_all
        env["sdo_all"] = sdo_all
        if STOP_AFTER not in ("B", "G"):
            _build_c(nc, env)
    if STOP_AFTER not in ("B", "G", "C"):
        _build_d(nc, env)


def _build_c(nc, env):
    S = env["S"]
    bar_in, bar_out = env["bar_in"], env["bar_out"]
    phase = env["phase"]
    bc_ap = env["bc_ap"]
    T, NCH = env["T"], env["NCH"]
    pYd, pYo, pAux = env["pYd"], env["pYo"], env["pAux"]
    ident_bf = env["ident_bf"]
    d_gate, d_yloc, d_norm = env["d_gate"], env["d_yloc"], env["d_norm"]
    nwv = env["nwv"]
    cx_all, sdo_all, E_bf = env["cx_all"], env["sdo_all"], env["E_bf"]
    pAux_bf = pAux[:].bitcast(BF16)

    with ExitStack() as sC:
        ec = sC.enter_context
        ybuf = ec(nc.sbuf_tensor("ybuf", [128, INTER], F32))
        gbuf = ec(nc.sbuf_tensor("gbuf", [128, INTER], BF16))
        gate_tm = ec(nc.sbuf_tensor("gate_tm", [128, INTER], BF16))
        xg = ec(nc.sbuf_tensor("xg", [128, INTER], BF16))
        nw_bf = ec(nc.sbuf_tensor("nw_bf", [128, INTER], BF16))
        normed_tm = ec(nc.sbuf_tensor("normed_tm", [128, INTER], BF16))
        ctv1 = ec(nc.sbuf_tensor("ctv1", [128, 1024], F32))
        ctv2 = ec(nc.sbuf_tensor("ctv2", [128, 1024], BF16))
        sqd = ec(nc.sbuf_tensor("sqd", [128, 1024], BF16))
        ms8 = ec(nc.sbuf_tensor("ms8", [128, 8], F32))
        msv = ec(nc.sbuf_tensor("msv", [128, 128], F32))
        l8 = ec(nc.sbuf_tensor("l8", [128, 128], F32))
        rstd8 = ec(nc.sbuf_tensor("rstd8", [128, 128], F32))



        with nc.Block("passC") as blk:

            @blk.sync
            def _(sy):
                bar_in(sy)
                for c in range(NCH):
                    T0 = 3 + 128 * c
                    if c >= 1:
                        sy.wait_ge(S["s_cev"].h, 8 * c)
                    S["s_ylc"].inc(sy.dma_start(out=ybuf[:], in_=d_yloc[c]), 16)
                    if c >= 1:
                        sy.wait_ge(S["s_nrm"].h, c)
                    S["s_glc"].inc(
                        sy.dma_start(
                            out=gbuf[:].rearrange("p (b t) -> p b t", b=64),
                            in_=d_gate[:, :, T0:T0 + 128]
                            .rearrange("b p t -> p b t")), 16)
                    sy.wait_ge(S["s_nrm"].h, c + 1)
                    S["s_nst"].inc(sy.dma_start(out=d_norm[c], in_=normed_tm[:]),
                                   16)
                sy.wait_ge(S["s_nst"].h, 16 * NCH)
                bar_out(sy)

            @blk.gpsimd
            def _(g):
                bar_in(g)
                g.memset(msv[:], 1.0)
                S["s_nwl"].inc(
                    g.dma_start(out=nw_bf[:], in_=bc_ap(nwv, 128, INTER)), 16)
                bar_out(g)

            @blk.tensor
            def _(t):
                bar_in(t)
                for c in range(NCH):
                    for gi in range(8):
                        t.wait_ge(S["s_glc"].h, 16 * (c + 1))
                        if 8 * c + gi >= 2:
                            t.wait_ge(S["s_gte"].h, 8 * c + gi - 1)
                        half = pAux_bf[:, 1024 * (gi % 2):1024 * (gi % 2) + 1024]
                        for j in range(8):
                            ins = t.transpose(
                                half[:, j * 128:(j + 1) * 128],
                                gbuf[:, (8 * gi + j) * 128:(8 * gi + j + 1) * 128],
                                ident_bf[:])
                        S["s_gtr"].inc(ins)
                    for hb in range(8):
                        pC = [pYd, pYo][hb % 2]
                        if 8 * c + hb >= 2:
                            t.wait_ge(S["s_cev"].h, 8 * c + hb - 1)
                        for i in range(16):
                            h = 16 * hb + i
                            ins = t.matmul(
                                pC[:, i * 64:(i + 1) * 64],
                                cx_all[:, c * 1024 + (i % 8) * 128:
                                       c * 1024 + (i % 8 + 1) * 128],
                                E_bf[:, h * 64:(h + 1) * 64],
                                start=True, stop=True)
                        S["s_corr"].inc(ins)
                bar_out(t)

            @blk.scalar
            def _(sc):
                bar_in(sc)
                sc.wait_ge(S["s_nwl"].h, 16)
                for c in range(NCH):
                    sc.wait_ge(S["s_rms1"].h, c + 1)
                    sc.activation(l8[:], msv[:], AF.Ln)
                    sc.activation(rstd8[:], l8[:], AF.Exp, scale=-0.5)
                    S["s_rmsa"].inc(sc.copy(l8[:], rstd8[:]))
                bar_out(sc)

            @blk.vector
            def _(v):
                bar_in(v)
                v.wait_ge(S["s_nwl"].h, 16)
                for c in range(NCH):
                    for gi in range(8):
                        v.wait_ge(S["s_gtr"].h, 8 * c + gi + 1)
                        half = pAux_bf[:, 1024 * (gi % 2):1024 * (gi % 2) + 1024]
                        S["s_gte"].inc(
                            v.tensor_copy(gate_tm[:, gi * 1024:(gi + 1) * 1024],
                                          half[:]))
                    for hb in range(8):
                        v.wait_ge(S["s_corr"].h, 8 * c + hb + 1)
                        v.wait_ge(S["s_ylc"].h, 16 * (c + 1))
                        pC = [pYd, pYo][hb % 2]
                        sv = (sdo_all[:, c * 128 + 16 * hb:c * 128 + 16 * hb + 16]
                              .unsqueeze(2).to_broadcast((128, 16, 64)))
                        v.tensor_tensor(
                            ctv1[:].rearrange("l (h p) -> l h p", h=16),
                            pC[:].rearrange("l (h p) -> l h p", h=16),
                            sv, ALU.mult)
                        v.tensor_tensor(ctv2[:], ctv1[:],
                                        ybuf[:, 1024 * hb:1024 * (hb + 1)],
                                        ALU.add)
                        S["s_cev"].inc(
                            v.tensor_tensor(xg[:, 1024 * hb:1024 * (hb + 1)],
                                            ctv2[:],
                                            gate_tm[:, 1024 * hb:1024 * (hb + 1)],
                                            ALU.mult))
                    for g2 in range(8):
                        v.tensor_tensor(sqd[:],
                                        xg[:, 1024 * g2:1024 * (g2 + 1)],
                                        xg[:, 1024 * g2:1024 * (g2 + 1)],
                                        ALU.mult)
                        v.reduce_sum(ms8[:, g2:g2 + 1], sqd[:], axis=AX.X)
                    v.tensor_copy(sqd[:], xg[:, 0:1024])
                    v.tensor_copy(sqd[:], xg[:, 1024:2048])
                    v.tensor_scalar(msv[:, 0:8], ms8[:], 1.0 / 1024.0,
                                    RMS_EPS, ALU.mult, ALU.add)
                    S["s_rms1"].inc(v.tensor_copy(sqd[:, 0:8], msv[:, 0:8]))
                    v.wait_ge(S["s_rmsa"].h, c + 1)
                    v.wait_ge(S["s_gtr"].h, 8 * c + 8)
                    if c >= 1:
                        v.wait_ge(S["s_nst"].h, 16 * c)
                    for g2 in range(8):
                        v.tensor_scalar(gbuf[:, 1024 * g2:1024 * (g2 + 1)],
                                        xg[:, 1024 * g2:1024 * (g2 + 1)],
                                        rstd8[:, g2:g2 + 1], None, ALU.mult)
                    S["s_nrm"].inc(
                        v.tensor_tensor(normed_tm[:], gbuf[:], nw_bf[:],
                                        ALU.mult))
                bar_out(v)

    phase[0] += 1


def _build_d(nc, env):
    S = env["S"]
    bar_in, bar_out = env["bar_in"], env["bar_out"]
    phase = env["phase"]
    T, tok = env["T"], env["tok"]
    TSW, NTS, NSUB = env["TSW"], env["NTS"], env["NSUB"]
    pYd, pYo = env["pYd"], env["pYo"]
    x_fm, wd, out = env["x_fm"], env["wd"], env["out"]
    d_ffnb, d_norm = env["d_ffnb"], env["d_norm"]
    t_sc, rr_col = env["t_sc"], env["rr_col"]

    pslots = [pYd[:, 0:512], pYd[:, 512:1024], pYo[:, 0:512], pYo[:, 512:1024]]

    with ExitStack() as sD:
        ed = sD.enter_context
        rhs_n = ed(nc.sbuf_tensor("rhs_n", [128, KB_D * TSW], BF16))
        wdbuf = [ed(nc.sbuf_tensor(f"wdbuf{i}", [128, KB_D * 128], BF16))
                 for i in range(2)]
        wlora = [ed(nc.sbuf_tensor(f"wlora{i}", [128, 128], BF16))
                 for i in range(2)]
        xbuf = ed(nc.sbuf_tensor("xbuf", [128, TSW], F32))
        fbuf = ed(nc.sbuf_tensor("fbuf", [128, TSW], F32))
        ostage = ed(nc.sbuf_tensor("ostage", [128, TSW], F32))
        dtv = ed(nc.sbuf_tensor("dtv", [128, 512], F32))

        with nc.Block("passD") as blk:

            @blk.sync
            def _(sy):
                bar_in(sy)
                for ts in range(NTS):
                    if ts >= 1:
                        sy.wait_ge(S["s_dmm"].h, 32 * NSUB * ts)
                    for kb in range(KB_D - 1):
                        S["s_rhs"].inc(
                            sy.dma_start_transpose(
                                rhs_n[:, kb * TSW:(kb + 1) * TSW],
                                d_norm.rearrange("c s f -> (c s) f")
                                [ts * TSW:(ts + 1) * TSW,
                                 kb * 128:(kb + 1) * 128]), 16)
                    for db in range(32):
                        gidx = 32 * ts + db
                        sy.wait_ge(S["s_dev"].h, NSUB * gidx)
                        S["s_xb"].inc(
                            sy.dma_start(
                                out=xbuf[:],
                                in_=x_fm[db * 128:(db + 1) * 128,
                                         3 + ts * TSW:3 + (ts + 1) * TSW]), 16)
                        if db >= 1:
                            sy.wait_ge(S["s_dev"].h, NSUB * gidx)
                            S["s_out"].inc(
                                sy.dma_start(
                                    out=out[db - 1][:, ts * TSW:(ts + 1) * TSW],
                                    in_=ostage[:]), 16)
                    sy.wait_ge(S["s_dev"].h, NSUB * (32 * ts + 32))
                    S["s_out"].inc(
                        sy.dma_start(out=out[31][:, ts * TSW:(ts + 1) * TSW],
                                     in_=ostage[:]), 16)
                sy.wait_ge(S["s_out"].h, 16 * 32 * NTS)
                bar_out(sy)

            @blk.gpsimd
            def _(g):
                bar_in(g)
                for ts in range(NTS):
                    for db in range(32):
                        gidx = 32 * ts + db
                        if gidx >= 2:
                            g.wait_ge(S["s_dmm"].h, NSUB * (gidx - 1))
                        wdsem = S["s_wd1"] if gidx % 2 else S["s_wd"]
                        wdsem.inc(
                            g.dma_start(
                                out=wdbuf[db % 2][:],
                                in_=wd.rearrange("(kb p) d -> p kb d", p=128)
                                [:, :, db * 128:(db + 1) * 128]), 16)
                        g.wait_ge(S["s_dev"].h, NSUB * gidx)
                        S["s_fb"].inc(
                            g.dma_start(
                                out=fbuf[:],
                                in_=d_ffnb[db][:, 3 + ts * TSW:
                                               3 + (ts + 1) * TSW]), 16)
                bar_out(g)

            @blk.tensor
            def _(t):
                bar_in(t)
                for ts in range(NTS):
                    t.wait_ge(S["s_rhs"].h, 16 * (KB_D - 1) * (ts + 1))
                    for db in range(32):
                        gidx = 32 * ts + db
                        t.wait_ge(S["s_wd1" if gidx % 2 else "s_wd"].h,
                                  16 * (gidx // 2 + 1))
                        t.wait_ge(S["s_wl"].h, gidx + 1)
                        for tsub in range(NSUB):
                            k = NSUB * gidx + tsub
                            if k >= 4:
                                t.wait_ge(S["s_dev"].h, k - 3)
                            ps = pslots[k % 4]
                            for kb in range(KB_D):
                                if kb == KB_D - 1:
                                    lhsT = wlora[db % 2][:]
                                    rhs = t_sc[:, 3 + ts * TSW + tsub * 512:
                                               3 + ts * TSW + tsub * 512 + 512]
                                else:
                                    lhsT = wdbuf[db % 2][:, kb * 128:(kb + 1) * 128]
                                    rhs = rhs_n[:, kb * TSW + tsub * 512:
                                                kb * TSW + tsub * 512 + 512]
                                ins = t.matmul(ps[:, 0:512], lhsT, rhs,
                                               start=(kb == 0),
                                               stop=(kb == KB_D - 1))
                            S["s_dmm"].inc(ins)
                bar_out(t)

            @blk.scalar
            def _(sc):
                bar_in(sc)
                bar_out(sc)

            @blk.vector
            def _(v):
                bar_in(v)
                for ts in range(NTS):
                    for db in range(32):
                        gidx = 32 * ts + db
                        v.wait_ge(S["s_wd1" if gidx % 2 else "s_wd"].h,
                                  16 * (gidx // 2 + 1))
                        if gidx >= 2:
                            v.wait_ge(S["s_dmm"].h, NSUB * (gidx - 1))
                        v.tensor_scalar(
                            wlora[db % 2][:],
                            wdbuf[db % 2][:, (KB_D - 1) * 128:KB_D * 128],
                            rr_col[:], None, ALU.mult)
                        S["s_wl"].inc(
                            v.tensor_copy(dtv[:, 0:64],
                                          wlora[db % 2][:].bitcast(F32)))
                        for tsub in range(NSUB):
                            k = NSUB * gidx + tsub
                            v.wait_ge(S["s_dmm"].h, k + 1)
                            v.wait_ge(S["s_xb"].h, 16 * (gidx + 1))
                            v.wait_ge(S["s_fb"].h, 16 * (gidx + 1))
                            if gidx >= 1 and tsub == 0:
                                v.wait_ge(S["s_out"].h, 16 * gidx)
                            ps = pslots[k % 4]
                            v.tensor_tensor(dtv[:], ps[:, 0:512],
                                            xbuf[:, tsub * 512:(tsub + 1) * 512],
                                            ALU.add)
                            S["s_dev"].inc(
                                v.tensor_tensor(
                                    ostage[:, tsub * 512:(tsub + 1) * 512],
                                    dtv[:],
                                    fbuf[:, tsub * 512:(tsub + 1) * 512],
                                    ALU.add))
                bar_out(v)

    phase[0] += 1


def _prep(inputs, tok):
    T = tok + 3
    f32 = np.float32
    x = np.asarray(inputs["L0_attn_hidden_states"], f32)[0]
    in_w = np.asarray(inputs["L0_attn_in_proj_w"], f32)
    conv_w = np.asarray(inputs["L0_attn_conv_w"], f32)[:, 0, :]
    conv_b = np.asarray(inputs["L0_attn_conv_b"], f32)
    dt_bias = np.asarray(inputs["L0_attn_dt_bias"], f32)
    A_log = np.asarray(inputs["L0_attn_A_log"], f32)
    Dv = np.asarray(inputs["L0_attn_D"], f32)
    nw = np.asarray(inputs["L0_attn_norm_weight"], f32)
    out_w = np.asarray(inputs["L0_attn_out_proj_w"], f32)
    fW = np.asarray(inputs["L0_ffn_W"], f32)
    fb = np.asarray(inputs["L0_ffn_b"], f32)
    fA = np.asarray(inputs["L0_ffn_A"], f32)
    fB = np.asarray(inputs["L0_ffn_B"], f32)
    rW = np.asarray(inputs["L0_ffn_router_W"], f32)
    rb = np.asarray(inputs["L0_ffn_router_b"], f32)

    ncore = NCORE
    xt = np.ascontiguousarray(x.T)
    router_pad = np.zeros((HID, 128), f32)
    router_pad[:, :8] = rW.T
    WA = np.ascontiguousarray(np.concatenate(
        [in_w.T, fA.reshape(128, HID).T, router_pad, fW.T], axis=1))
    WD = np.ascontiguousarray(np.concatenate(
        [out_w.T, fB.transpose(0, 2, 1).reshape(128, HID)], axis=0))
    cwP = np.ascontiguousarray(
        conv_w.reshape(NB_HBC, 128, KW).transpose(1, 0, 2).reshape(128, NB_HBC * KW))
    cbP = np.ascontiguousarray(conv_b.reshape(NB_HBC, 128).T)
    bpP = np.ascontiguousarray(fb.reshape(ND, 128).T)

    in_maps = []
    for i in range(ncore):
        s0 = i * tok
        xi = np.zeros((HID, T), f32)
        lo = max(0, s0 - 3)
        xi[:, 3 - (s0 - lo):] = xt[:, lo:s0 + tok]
        rm = np.zeros((1, T), f32)
        for tloc in range(3, T):
            gtok = s0 + tloc - 3
            if Q_LO <= gtok < Q_HI:
                rm[0, tloc] = 1.0 / NQ
        rm8 = np.broadcast_to(rm, (8, T)).copy()
        lmv = np.zeros((128, 8), f32)
        lmv[:, :i] = 1.0
        in_maps.append({
            "x": np.ascontiguousarray(xi),
            "wa": WA, "wd": WD, "cw": cwP, "cb": cbP,
            "dtb": np.ascontiguousarray(dt_bias.reshape(128, 1)),
            "alog": np.ascontiguousarray(A_log.reshape(128, 1)),
            "dv": np.ascontiguousarray(Dv.reshape(128, 1)),
            "nw": np.ascontiguousarray(nw.reshape(1, INTER)),
            "bp": bpP,
            "rb": np.ascontiguousarray(rb.reshape(1, 8)),
            "rmask": rm8,
            "lmask": np.ascontiguousarray(lmv),
            "ilmask": np.ascontiguousarray(1.0 - lmv),
        })
    return in_maps


_NC_CACHE = {}


def kernel(**inputs):
    tok = SEQ // NCORE
    in_maps = _prep(inputs, tok)
    key = tok
    if key not in _NC_CACHE:
        _NC_CACHE[key] = build(tok=tok)
    nc = _NC_CACHE[key]
    res = run_bass_kernel_spmd(nc, in_maps, core_ids=list(range(NCORE)))
    full = np.empty((SEQ, HID), np.float32)
    for i in range(NCORE):
        o = np.asarray(res.results[i]["out"])
        full[i * tok:(i + 1) * tok, :] = o.reshape(HID, tok).T
    return full[None]


# version 03 (final)
# speedup vs baseline: 1.0340x; 1.0340x over previous
import sys
import numpy as np
from contextlib import ExitStack

try:
    import concourse.bass as bass
except ImportError:
    sys.path.insert(0, "/opt/trn_rl_repo")
    import concourse.bass as bass
import concourse.mybir as mybir
from concourse.bass_utils import run_bass_kernel_spmd

F32 = mybir.dt.float32
BF16 = mybir.dt.bfloat16
AF = mybir.ActivationFunctionType
ALU = mybir.AluOpType
AX = mybir.AxisListType

HID = 4096
NKB = 32
INTER = 8192
NB_GATE, NB_HBC, NB_FFN = 64, 80, 32
NCB = 179
CB_DT, CB_LORA, CB_RT = 144, 145, 146
CB_ORDER = [CB_RT, CB_DT, CB_LORA] + list(range(0, 144)) + list(range(147, 179))
KB_D = 65
ND = 32
KW = 4
LN_EPS = 1e-5
RMS_EPS = 1e-5
Q_LO, Q_HI = 611, 16383
NQ = Q_HI - Q_LO
LORA_SCALE = 2.0
NCORE = 8
SEQ = 16384
STOP_AFTER = None


class TSem:
    def __init__(self, es, nc, name):
        self.h = es.enter_context(nc.semaphore(name))
        self.n = 0

    def inc(self, ins, k=1):
        ins.then_inc(self.h, k)
        self.n += k
        return ins


def build(tok=2048, taps=()):
    T = tok + 3
    NCH = tok // 128
    TTS = []
    r = T
    while r > 0:
        TTS.append(min(411, r))
        r -= TTS[-1]
    NTT = len(TTS)
    TTM = max(TTS)
    TSW = min(1024, tok)
    NTS = tok // TSW
    NSUB = TSW // 512

    nc = bass.Bass("TRN2", target_bir_lowering=False,
                   detect_race_conditions=False)

    def din(name, shape, dt=F32):
        return nc.dram_tensor(name, shape, dt, kind="ExternalInput").ap()

    x_fm = din("x", [HID, T])
    wa = din("wa", [HID, NCB * 128])
    wd = din("wd", [KB_D * 128, HID])
    cw = din("cw", [128, NB_HBC * KW])
    cbv = din("cb", [128, NB_HBC])
    dtb = din("dtb", [128, 1])
    alog = din("alog", [128, 1])
    dvec = din("dv", [128, 1])
    nwv = din("nw", [1, INTER])
    bpv = din("bp", [128, ND])
    rbv = din("rb", [1, 8])
    rmask = din("rmask", [8, T])
    lmask = din("lmask", [128, 8])
    ilmask = din("ilmask", [128, 8])

    out = nc.dram_tensor("out", [ND, 128, tok], F32, kind="ExternalOutput").ap()

    def scr(name, shape, dt):
        kind = "ExternalOutput" if name in taps else "Internal"
        return nc.dram_tensor(name, shape, dt, kind=kind).ap()

    d_gate = scr("d_gate", [NB_GATE, 128, T], BF16)
    d_hbc = scr("d_hbc", [NB_HBC, 128, T], BF16)
    d_ffnb = scr("d_ffnb", [NB_FFN, 128, T], BF16)
    d_yloc = scr("d_yloc", [NCH, 128, INTER], F32)
    d_norm = scr("d_norm", [NCH, 128, INTER], BF16)
    d_acs = scr("d_acs", [NCH, 128, 128], F32)
    d_rstd = scr("d_rstd", [1, T], F32)
    d_mr = scr("d_mr", [1, T], F32)
    d_cd = scr("d_cd", [128, 1], F32)
    d_dtot = scr("d_dtot", [128, 1], F32)
    d_dst = scr("d_dst", [128, 1], F32)
    d_rt = scr("d_rt", [1, 8], F32)
    d_sbf = scr("d_sbf", [128, INTER], BF16) if "d_sbf" in taps else None
    d_ebf = scr("d_ebf", [128, INTER], BF16) if "d_ebf" in taps else None
    d_cco = scr("d_cco", [NCORE * 128, INTER + 1], BF16) if "d_cco" in taps else None
    cc_in = nc.dram_tensor("cc_in", [128, INTER + 1], BF16, kind="Internal").ap()
    cc_out = nc.dram_tensor("cc_out", [NCORE * 128, INTER + 1], BF16,
                            kind="Internal", addr_space="Shared").ap()
    ar_in = nc.dram_tensor("ar_in", [1, 8], F32, kind="Internal").ap()
    ar_out = nc.dram_tensor("ar_out", [1, 8], F32, kind="Internal").ap()

    def bc_ap(dap, npart, nfree):
        return bass.AP(dap.tensor, 0, [[0, npart], [1, nfree]])

    top = ExitStack()
    with top:
        et = top.enter_context

        dt_raw = et(nc.sbuf_tensor("dt_raw", [128, T], F32))
        t_sc = et(nc.sbuf_tensor("t_sc", [128, T], BF16))
        cdlog = [et(nc.sbuf_tensor(f"cdlog{i}", [128, 1], F32)) for i in range(2)]
        ident_bf = et(nc.sbuf_tensor("ident_bf", [128, 128], BF16))
        ident_f = et(nc.sbuf_tensor("ident_f", [128, 128], F32))
        trimask = et(nc.sbuf_tensor("trimask", [128, 128], F32))
        ones128 = et(nc.sbuf_tensor("ones128", [128, 128], F32))
        cw_bf = et(nc.sbuf_tensor("cw_bf", [128, NB_HBC * KW], BF16))
        cb_bf = et(nc.sbuf_tensor("cb_bf", [128, NB_HBC], BF16))
        dtb_t = et(nc.sbuf_tensor("dtb_t", [128, 1], F32))
        acol = et(nc.sbuf_tensor("acol", [128, 1], F32))
        alog_t = et(nc.sbuf_tensor("alog_t", [128, 1], F32))
        dbc = et(nc.sbuf_tensor("dbc", [128, 128], F32))
        bp_t = et(nc.sbuf_tensor("bp_t", [128, ND], F32))
        rb_t = et(nc.sbuf_tensor("rb_t", [1, 8], F32))
        lm_t = et(nc.sbuf_tensor("lm_t", [128, 8], F32))
        ilm_t = et(nc.sbuf_tensor("ilm_t", [128, 8], F32))
        rr_col = et(nc.sbuf_tensor("rr_col", [128, 1], F32))
        onecol = et(nc.sbuf_tensor("onecol", [128, 1], BF16))

        pYd = et(nc.psum_tensor("pYd", [128, 1024], F32))
        pYo = et(nc.psum_tensor("pYo", [128, 1024], F32))
        pF = et(nc.psum_tensor("pF", [128, 1024], F32))
        pAux = et(nc.psum_tensor("pAux", [128, 1024], F32))

        SN = ("bar s_sml s_x s_sq s_smm s_stats s_stev s_ln1 s_lna s_lnro "
              "s_lnbc s_ln2 s_xn s_w s_mm s_evv s_eva s_st s_rt s_ari s_cc s_aro "
              "s_rt2 s_rt3 s_rt4 s_rt5 s_setup "
              "s_hbcl s_conv s_silu s_dt1 s_dta s_acs s_cdu s_acso s_trnA "
              "s_trevA s_trnH s_trevH s_gmt s_gmm s_cde s_dtt s_sm1 s_sm2 "
              "s_sm3 s_sm4 s_dec s_sdr s_sdo s_rbc s_mtd s_mte s_mt s_bd "
              "s_yd s_yo s_f s_yev s_sup s_yst "
              "s_blob s_cc2 s_gt s_dstg s_dbc s_ecomb "
              "s_ylc s_glc s_gtr s_gte s_corr s_cev s_rms1 s_rmsa s_nrm "
              "s_nst s_rhs s_wd s_wl s_fb s_xb s_dmm s_dev s_out s_nwl s_w1 s_wd1 "
              "s_lnr2 s_tap s_gt1 s_rh0 s_rh1 s_rh2 s_rh3").split()
        S = {sn: TSem(top, nc, sn) for sn in SN}

        NENG = 5
        phase = [0]

        def bar_in(eng):
            if phase[0] > 0:
                eng.wait_ge(S["bar"].h, NENG * phase[0])

        def bar_out(eng):
            S["bar"].inc(eng.nop())

        ev_sched = []
        _cv, _ca = [0], [0]
        for cb in CB_ORDER:
            for tt in range(NTT):
                if 0 <= cb < NB_GATE:
                    _ca[0] += 1
                    ev_sched.append(("s_eva", _ca[0]))
                else:
                    _cv[0] += 1
                    ev_sched.append(("s_evv", _cv[0]))
        staged = [cb for cb in CB_ORDER if cb < 144 or cb >= 147]
        st_idx = {cb: i for i, cb in enumerate(staged)}
        CB_POS = {cb: i for i, cb in enumerate(CB_ORDER)}

        def pass_a():
            with ExitStack() as sA:
                ea = sA.enter_context
                xnreg = ea(nc.sbuf_tensor("xnreg", [128, NKB * T], BF16))
                wbuf = [ea(nc.sbuf_tensor(f"wbuf{i}", [128, NKB * 128], BF16))
                        for i in range(2)]
                stage = ea(nc.sbuf_tensor("stg", [128, T], BF16))
                x2t = [ea(nc.sbuf_tensor(f"x2t{i}", [128, TTM], BF16))
                       for i in range(2)]
                r_s1 = ea(nc.sbuf_tensor("r_s1", [1, TTM], F32))
                r_s2 = ea(nc.sbuf_tensor("r_s2", [1, TTM], F32))
                r_m = ea(nc.sbuf_tensor("r_m", [1, TTM], F32))
                r_t = ea(nc.sbuf_tensor("r_t", [1, TTM], F32))
                r_rstd = ea(nc.sbuf_tensor("r_rstd", [1, TTM], F32))
                r_mr = ea(nc.sbuf_tensor("r_mr", [1, TTM], F32))
                rbc_bf = ea(nc.sbuf_tensor("rbc_bf", [128, T], BF16))
                mrb_bf = ea(nc.sbuf_tensor("mrb_bf", [128, T], BF16))
                nstg = ea(nc.sbuf_tensor("nstg", [128, T], BF16))
                tmp8 = ea(nc.sbuf_tensor("tmp8", [8, TTM], F32))
                red8 = ea(nc.sbuf_tensor("red8", [8, 1], F32))
                lsum = ea(nc.sbuf_tensor("lsum", [8, 1], F32))
                rrow = ea(nc.sbuf_tensor("rrow", [1, 8], F32))
                erow = ea(nc.sbuf_tensor("erow", [1, 8], F32))
                rmx = ea(nc.sbuf_tensor("rmx", [1, 1], F32))
                rsm = ea(nc.sbuf_tensor("rsm", [1, 1], F32))
                rrec = ea(nc.sbuf_tensor("rrec", [1, 1], F32))
                rmask_t = ea(nc.sbuf_tensor("rmask_t", [8, T], F32))

                slots = [pYd[:, 0:512], pYd[:, 512:1024], pYo[:, 0:512],
                         pYo[:, 512:1024], pF[:, 0:512], pF[:, 512:1024]]

                with nc.Block("passA") as blk:

                    @blk.sync
                    def _(sy):
                        bar_in(sy)
                        for dst, src in ((dtb_t, dtb), (alog_t, alog),
                                         (rb_t, rbv), (rmask_t, rmask),
                                         (lm_t, lmask), (ilm_t, ilmask),
                                         (bp_t, bpv)):
                            S["s_sml"].inc(sy.dma_start(out=dst[:], in_=src), 16)
                        S["s_sml"].inc(
                            sy.dma_start(out=dbc[:], in_=bc_ap(dvec, 128, 128)), 16)
                        off = 0
                        for tt, w in enumerate(TTS):
                            sy.wait_ge(S["s_lna"].h, tt + 1)
                            S["s_lnro"].inc(
                                sy.dma_start(out=d_rstd[:, off:off + w],
                                             in_=r_rstd[:, 0:w]), 16)
                            sy.wait_ge(S["s_ln2"].h, tt + 1)
                            S["s_lnr2"].inc(
                                sy.dma_start(out=d_mr[:, off:off + w],
                                             in_=r_mr[:, 0:w]), 16)
                            off += w
                        sy.wait_ge(S["s_cc"].h, 1)
                        S["s_aro"].inc(sy.dma_start(out=rrow[:], in_=ar_out), 16)
                        sy.wait_ge(S["s_rt3"].h, 2)
                        S["s_rt4"].inc(sy.dma_start(out=d_rt, in_=rrow[:]), 16)
                        sy.wait_ge(S["s_rt4"].h, 16)
                        S["s_rt5"].inc(
                            sy.dma_start(out=rr_col[:],
                                         in_=bass.AP(d_rt.tensor, 0,
                                                     [[1, 8], [0, 16], [0, 1]])),
                            16)
                        for i, cb in enumerate(staged):
                            sem, val = ev_sched[(CB_POS[cb] + 1) * NTT - 1]
                            sy.wait_ge(S[sem].h, val)
                            if cb < NB_GATE:
                                dst = d_gate[cb]
                            elif cb < 144:
                                dst = d_hbc[cb - NB_GATE]
                            else:
                                dst = d_ffnb[cb - 147]
                            S["s_st"].inc(sy.dma_start(out=dst, in_=stage[:]), 16)
                        sy.wait_ge(S["s_st"].h, 16 * len(staged))
                        sy.wait_ge(S["s_rt5"].h, 16)
                        bar_out(sy)

                    @blk.gpsimd
                    def _(g):
                        bar_in(g)
                        g.memset(ident_bf[:], 0.0)
                        g.affine_select(out=ident_bf[:], in_=ident_bf[:],
                                        compare_op=ALU.not_equal, fill=1.0,
                                        base=0, pattern=[[-1, 128]],
                                        channel_multiplier=1)
                        g.memset(ident_f[:], 0.0)
                        g.affine_select(out=ident_f[:], in_=ident_f[:],
                                        compare_op=ALU.not_equal, fill=1.0,
                                        base=0, pattern=[[-1, 128]],
                                        channel_multiplier=1)
                        g.memset(trimask[:], 1.0)
                        g.affine_select(out=trimask[:], in_=trimask[:],
                                        compare_op=ALU.is_ge, fill=0.0, base=0,
                                        pattern=[[1, 128]],
                                        channel_multiplier=-1)
                        g.memset(ones128[:], 1.0)
                        g.memset(onecol[:], 1.0)
                        g.memset(cdlog[0][:], 0.0)
                        S["s_setup"].inc(g.nop())
                        for kb in range(NKB):
                            S["s_x"].inc(
                                g.dma_start(out=xnreg[:, kb * T:(kb + 1) * T],
                                            in_=x_fm[kb * 128:(kb + 1) * 128, :]),
                                16)
                        S["s_tap"].inc(g.dma_start(out=cw_bf[:], in_=cw), 16)
                        S["s_tap"].inc(g.dma_start(out=cb_bf[:], in_=cbv), 16)
                        g.wait_ge(S["s_lnro"].h, 16 * NTT)
                        g.wait_ge(S["s_lnr2"].h, 16 * NTT)
                        S["s_lnbc"].inc(
                            g.dma_start(out=rbc_bf[:], in_=bc_ap(d_rstd, 128, T)),
                            16)
                        S["s_lnbc"].inc(
                            g.dma_start(out=mrb_bf[:], in_=bc_ap(d_mr, 128, T)),
                            16)
                        for idx, cb in enumerate(CB_ORDER):
                            if idx >= 2:
                                g.wait_ge(S["s_mm"].h, (idx - 1) * NTT)
                            wsem = S["s_w1"] if idx % 2 else S["s_w"]
                            wsem.inc(
                                g.dma_start(
                                    out=wbuf[idx % 2][:],
                                    in_=wa.rearrange("(kb p) c -> p kb c", p=128)
                                    [:, :, cb * 128:(cb + 1) * 128]), 16)
                            if idx == 3:
                                g.wait_ge(S["s_rt"].h, 1)
                                S["s_ari"].inc(
                                    g.dma_start(out=ar_in, in_=lsum[:]), 16)
                                g.wait_ge(S["s_ari"].h, 16)
                                S["s_cc"].inc(
                                    g.collective_compute(
                                        "AllReduce", ALU.add,
                                        replica_groups=[list(range(NCORE))],
                                        ins=[ar_in], outs=[ar_out]))
                        bar_out(g)

                    @blk.tensor
                    def _(t):
                        bar_in(t)
                        t.wait_ge(S["s_setup"].h, 1)
                        off = 0
                        for tt, w in enumerate(TTS):
                            if tt > 0:
                                t.wait_ge(S["s_stev"].h, tt)
                            for kb in range(NKB):
                                t.wait_ge(S["s_x"].h, 16 * NKB)
                                t.matmul(pAux[0:1, 0:w], onecol[:],
                                         xnreg[:, kb * T + off: kb * T + off + w],
                                         start=(kb == 0), stop=(kb == NKB - 1))
                                t.wait_ge(S["s_sq"].h, tt * NKB + kb + 1)
                                ins = t.matmul(pAux[0:1, 512:512 + w], onecol[:],
                                               x2t[kb % 2][:, 0:w],
                                               start=(kb == 0),
                                               stop=(kb == NKB - 1))
                                if kb < NKB - 1:
                                    S["s_smm"].inc(ins)
                            S["s_stats"].inc(ins)
                            off += w
                        t.wait_ge(S["s_xn"].h, NKB)
                        gi = 0
                        for idx, cb in enumerate(CB_ORDER):
                            t.wait_ge(S["s_w1" if idx % 2 else "s_w"].h,
                                      16 * (idx // 2 + 1))
                            off = 0
                            for tt, w in enumerate(TTS):
                                if gi >= 6:
                                    sem, val = ev_sched[gi - 6]
                                    t.wait_ge(S[sem].h, val)
                                ps = slots[gi % 6]
                                for kb in range(NKB):
                                    ins = t.matmul(
                                        ps[:, 0:w],
                                        wbuf[idx % 2][:, kb * 128:(kb + 1) * 128],
                                        xnreg[:, kb * T + off: kb * T + off + w],
                                        start=(kb == 0), stop=(kb == NKB - 1))
                                S["s_mm"].inc(ins)
                                gi += 1
                                off += w
                        bar_out(t)

                    @blk.scalar
                    def _(sc):
                        bar_in(sc)
                        def _lnrow(tt):
                            sc.wait_ge(S["s_ln1"].h, tt + 1)
                            if tt > 0:
                                sc.wait_ge(S["s_lnro"].h, 16 * tt)
                            sc.activation(r_s2[:, 0:TTM], r_t[:], AF.Ln)
                            S["s_lna"].inc(
                                sc.activation(r_rstd[:], r_s2[:, 0:TTM], AF.Exp,
                                              scale=-0.5))

                        off = 0
                        for tt, w in enumerate(TTS):
                            for kb in range(NKB):
                                sc.wait_ge(S["s_x"].h, 16 * NKB)
                                if kb >= 2:
                                    sc.wait_ge(S["s_smm"].h,
                                               tt * (NKB - 1) + kb - 1)
                                elif kb == 0 and tt > 0:
                                    sc.wait_ge(S["s_smm"].h, tt * (NKB - 1))
                                elif kb == 1 and tt > 0:
                                    sc.wait_ge(S["s_stats"].h, tt)
                                S["s_sq"].inc(
                                    sc.activation(
                                        x2t[kb % 2][:, 0:w],
                                        xnreg[:, kb * T + off: kb * T + off + w],
                                        AF.Square))
                            off += w
                            if tt >= 1:
                                _lnrow(tt - 1)
                        _lnrow(NTT - 1)
                        gi = 0
                        for idx, cb in enumerate(CB_ORDER):
                            if idx == 4:
                                sc.wait_ge(S["s_rt2"].h, 1)
                                sc.activation(erow[:], rrow[:], AF.Exp)
                                S["s_rt3"].inc(sc.copy(rrow[:], erow[:]))
                            off = 0
                            for tt, w in enumerate(TTS):
                                if 0 <= cb < NB_GATE:
                                    sc.wait_ge(S["s_mm"].h, gi + 1)
                                    si = st_idx[cb]
                                    if tt == 0 and si >= 1:
                                        sc.wait_ge(S["s_st"].h, 16 * si)
                                    S["s_eva"].inc(
                                        sc.activation(stage[:, off:off + w],
                                                      slots[gi % 6][:, 0:w],
                                                      AF.Silu))
                                gi += 1
                                off += w
                        bar_out(sc)

                    @blk.vector
                    def _(v):
                        bar_in(v)
                        v.wait_ge(S["s_sml"].h, 16 * 8)
                        for tt, w in enumerate(TTS):
                            v.wait_ge(S["s_stats"].h, tt + 1)
                            v.tensor_copy(r_s1[:, 0:w], pAux[0:1, 0:w])
                            S["s_stev"].inc(
                                v.tensor_copy(r_s2[:, 0:w],
                                              pAux[0:1, 512:512 + w]))
                            v.tensor_scalar(r_m[:, 0:w], r_s1[:, 0:w],
                                            1.0 / HID, None, ALU.mult)
                            v.tensor_tensor(r_t[:, 0:w], r_m[:, 0:w],
                                            r_m[:, 0:w], ALU.mult)
                            v.scalar_tensor_tensor(r_s1[:, 0:w], r_s2[:, 0:w],
                                                   1.0 / HID, r_t[:, 0:w],
                                                   ALU.mult, ALU.subtract)
                            v.tensor_scalar(r_t[:, 0:w], r_s1[:, 0:w],
                                            LN_EPS, None, ALU.add)
                            S["s_ln1"].inc(
                                v.tensor_copy(r_s1[:, 0:w], r_t[:, 0:w]))
                            v.wait_ge(S["s_lna"].h, tt + 1)
                            if tt > 0:
                                v.wait_ge(S["s_lnr2"].h, 16 * tt)
                            S["s_ln2"].inc(
                                v.tensor_tensor(r_mr[:, 0:w], r_m[:, 0:w],
                                                r_rstd[:, 0:w], ALU.mult))
                        v.wait_ge(S["s_sq"].h, NTT * NKB)
                        v.wait_ge(S["s_stats"].h, NTT)
                        v.wait_ge(S["s_lnbc"].h, 32)
                        for kb in range(NKB):
                            v.tensor_tensor(nstg[:], xnreg[:, kb * T:(kb + 1) * T],
                                            rbc_bf[:], ALU.mult)
                            S["s_xn"].inc(
                                v.tensor_tensor(xnreg[:, kb * T:(kb + 1) * T],
                                                nstg[:], mrb_bf[:],
                                                ALU.subtract))
                        gi = 0
                        for idx, cb in enumerate(CB_ORDER):
                            off = 0
                            for tt, w in enumerate(TTS):
                                ps = slots[gi % 6]
                                if not (0 <= cb < NB_GATE):
                                    v.wait_ge(S["s_mm"].h, gi + 1)
                                    if cb == CB_DT:
                                        ins = v.tensor_copy(
                                            dt_raw[:, off:off + w], ps[:, 0:w])
                                    elif cb == CB_LORA:
                                        ins = v.tensor_copy(
                                            t_sc[:, off:off + w], ps[:, 0:w])
                                    elif cb == CB_RT:
                                        ins = v.tensor_tensor(
                                            tmp8[:, 0:w], ps[0:8, 0:w],
                                            rmask_t[:, off:off + w], ALU.mult)
                                        v.reduce_sum(red8[:], tmp8[:, 0:w],
                                                     axis=AX.X)
                                        v.tensor_copy(tmp8[:, 0:w],
                                                      rmask_t[:, off:off + w])
                                        v.tensor_copy(tmp8[:, 0:w],
                                                      rmask_t[:, off:off + w])
                                        if tt == 0:
                                            v.tensor_copy(lsum[:], red8[:])
                                        else:
                                            ins2 = v.tensor_tensor(
                                                lsum[:], lsum[:], red8[:],
                                                ALU.add)
                                            if tt == NTT - 1:
                                                S["s_rt"].inc(ins2)
                                    else:
                                        si = st_idx[cb]
                                        if tt == 0 and si >= 1:
                                            v.wait_ge(S["s_st"].h, 16 * si)
                                        if cb >= 147:
                                            ins = v.tensor_scalar(
                                                stage[:, off:off + w],
                                                ps[:, 0:w],
                                                bp_t[:, cb - 147:cb - 146],
                                                None, ALU.add)
                                        else:
                                            ins = v.tensor_copy(
                                                stage[:, off:off + w],
                                                ps[:, 0:w])
                                    S["s_evv"].inc(ins)
                                gi += 1
                                off += w
                            if idx == 8:
                                v.wait_ge(S["s_aro"].h, 16)
                                v.tensor_tensor(erow[:], rrow[:], rb_t[:],
                                                ALU.add)
                                v.reduce_max(rmx[:], erow[:], axis=AX.X)
                                v.tensor_copy(tmp8[:], rmask_t[:, 0:TTM])
                                v.tensor_copy(tmp8[:], rmask_t[:, 0:TTM])
                                v.tensor_scalar(rrow[:], erow[:], rmx[:],
                                                None, ALU.subtract)
                                S["s_rt2"].inc(v.tensor_copy(erow[:], rrow[:]))
                            if idx == 12:
                                v.wait_ge(S["s_rt3"].h, 1)
                                v.reduce_sum(rsm[:], erow[:], axis=AX.X)
                                v.tensor_copy(tmp8[:], rmask_t[:, 0:TTM])
                                v.tensor_copy(tmp8[:], rmask_t[:, 0:TTM])
                                v.reciprocal(rrec[:], rsm[:])
                                v.tensor_copy(tmp8[:], rmask_t[:, 0:TTM])
                                v.tensor_copy(tmp8[:], rmask_t[:, 0:TTM])
                                v.tensor_copy(tmp8[:], rmask_t[:, 0:TTM])
                                S["s_rt3"].inc(
                                    v.tensor_scalar(rrow[:], erow[:], rrec[:],
                                                    LORA_SCALE, ALU.mult,
                                                    ALU.mult))
                        bar_out(v)

            phase[0] += 1

        pass_a()
        if STOP_AFTER != "A":
            _build_bgcd(nc, locals())
    return nc


def _build_bgcd(nc, env):
    S = env["S"]
    bar_in, bar_out = env["bar_in"], env["bar_out"]
    phase = env["phase"]
    bc_ap = env["bc_ap"]
    T, tok, NCH, taps = env["T"], env["tok"], env["NCH"], env["taps"]
    TSW, NTS, NSUB = env["TSW"], env["NTS"], env["NSUB"]
    pYd, pYo, pF, pAux = env["pYd"], env["pYo"], env["pF"], env["pAux"]
    dt_raw, t_sc, cdlog = env["dt_raw"], env["t_sc"], env["cdlog"]
    ident_bf, ident_f = env["ident_bf"], env["ident_f"]
    trimask, ones128 = env["trimask"], env["ones128"]
    cw_bf, cb_bf = env["cw_bf"], env["cb_bf"]
    dtb_t, acol, alog_t, dbc = env["dtb_t"], env["acol"], env["alog_t"], env["dbc"]
    lm_t, ilm_t, rr_col = env["lm_t"], env["ilm_t"], env["rr_col"]
    d_gate, d_hbc, d_ffnb = env["d_gate"], env["d_hbc"], env["d_ffnb"]
    d_yloc, d_norm, d_acs = env["d_yloc"], env["d_norm"], env["d_acs"]
    d_cd, d_dtot, d_dst = env["d_cd"], env["d_dtot"], env["d_dst"]
    cc_in, cc_out = env["cc_in"], env["cc_out"]
    x_fm, wd, nwv, out = env["x_fm"], env["wd"], env["nwv"], env["out"]
    d_sbf, d_ebf = env["d_sbf"], env["d_ebf"]
    d_cco = env["d_cco"]

    pAux_bf = pAux[:].bitcast(BF16)

    with ExitStack() as sBC:
        eb = sBC.enter_context
        cx_all = eb(nc.sbuf_tensor("cx_all", [128, NCH * 1024], BF16))
        sdo_all = eb(nc.sbuf_tensor("sdo_all", [128, NCH * 128], F32))
        s_bf = eb(nc.sbuf_tensor("s_bf", [128, INTER], BF16))

        with ExitStack() as sB:
            e2 = sB.enter_context
            hbuf = e2(nc.sbuf_tensor("hbuf", [128, 20 * 131], BF16))
            qa = e2(nc.sbuf_tensor("qa", [128, 2560], BF16))
            qb = e2(nc.sbuf_tensor("qb", [128, 2560], BF16))
            qc = e2(nc.sbuf_tensor("qc", [128, 2560], BF16))
            qd = e2(nc.sbuf_tensor("qd", [128, 2560], BF16))
            cs2 = e2(nc.sbuf_tensor("cs2", [128, 72 * 128], BF16))
            sdt = e2(nc.sbuf_tensor("sdt", [128, 128], F32))
            mdt = e2(nc.sbuf_tensor("mdt", [128, 128], F32))
            adt = e2(nc.sbuf_tensor("adt", [128, 128], F32))
            edt = e2(nc.sbuf_tensor("edt", [128, 128], F32))
            ldt = e2(nc.sbuf_tensor("ldt", [128, 128], F32))
            dtc = e2(nc.sbuf_tensor("dtc", [128, 128], F32))
            admt = e2(nc.sbuf_tensor("admt", [128, 128], F32))
            a_cs = e2(nc.sbuf_tensor("a_cs", [128, 128], F32))
            a_csT = e2(nc.sbuf_tensor("a_csT", [128, 128], F32))
            dtT_f = e2(nc.sbuf_tensor("dtT_f", [128, 128], F32))
            dtT_bf = e2(nc.sbuf_tensor("dtT_bf", [128, 128], BF16))
            decT_bf = e2(nc.sbuf_tensor("decT_bf", [128, 128], BF16))
            ddtT_bf = e2(nc.sbuf_tensor("ddtT_bf", [128, 128], BF16))
            rdt = e2(nc.sbuf_tensor("rdt", [128, 128], F32))
            dec_fm = e2(nc.sbuf_tensor("dec_fm", [128, 128], F32))
            sdoT_raw = e2(nc.sbuf_tensor("sdoT_raw", [128, 128], F32))
            cdb = e2(nc.sbuf_tensor("cdb", [128, 128], F32))
            dtotb = e2(nc.sbuf_tensor("dtotb", [128, 128], BF16))
            cdxp = e2(nc.sbuf_tensor("cdxp", [128, 1], F32))
            dttc = e2(nc.sbuf_tensor("dttc", [128, 1], F32))
            x_tm = e2(nc.sbuf_tensor("x_tm", [128, INTER], BF16))
            bx_tm = e2(nc.sbuf_tensor("bx_tm", [128, 1024], BF16))
            rbcb = e2(nc.sbuf_tensor("rbcb", [128, 2048], F32))
            dmt = e2(nc.sbuf_tensor("dmt", [128, 2048], F32))
            dmc = e2(nc.sbuf_tensor("dmc", [128, 2048], F32))
            ebuf = e2(nc.sbuf_tensor("ebuf", [128, 2048], BF16))
            mtt = [e2(nc.sbuf_tensor(f"mtt{i}", [128, 2048], BF16))
                   for i in range(2)]
            bdt = [e2(nc.sbuf_tensor(f"bdt{i}", [128, 2048], BF16))
                   for i in range(2)]
            gmm = e2(nc.sbuf_tensor("gmm", [128, 1024], BF16))
            tev1 = e2(nc.sbuf_tensor("tev1", [128, 1024], F32))
            tev2 = e2(nc.sbuf_tensor("tev2", [128, 1024], BF16))
            dres = e2(nc.sbuf_tensor("dres", [128, 1024], BF16))
            ystage = [e2(nc.sbuf_tensor(f"ystage{i}", [128, 1024], F32))
                      for i in range(2)]
            dumpV = e2(nc.sbuf_tensor("dumpV", [128, 128], F32))
            dumpA = e2(nc.sbuf_tensor("dumpA", [128, 128], F32))

            hview = hbuf[:].rearrange("p (b t) -> p b t", b=20)
            cwv = cw_bf[:].rearrange("p (b k) -> p b k", k=KW)

            with nc.Block("passB") as blk:

                @blk.sync
                def _(sy):
                    bar_in(sy)
                    for c in range(NCH):
                        base = 128 * c
                        for q in range(4):
                            sy.wait_ge(S["s_conv"].h, 4 * c + q)
                            S["s_hbcl"].inc(
                                sy.dma_start(
                                    out=hview,
                                    in_=d_hbc[20 * q:20 * q + 20, :,
                                              base:base + 131]
                                    .rearrange("b p t -> p b t")), 16)
                        sy.wait_ge(S["s_acs"].h, c + 1)
                        S["s_acso"].inc(sy.dma_start(out=d_acs[c], in_=a_cs[:]), 16)
                        sy.wait_ge(S["s_cde"].h, c + 1)
                        S["s_sm1"].inc(sy.dma_start(out=d_cd, in_=cdxp[:]), 16)
                        sy.wait_ge(S["s_sm1"].h, 16 * (c + 1))
                        if c > 0:
                            sy.wait_ge(S["s_sdo"].h, c)
                        S["s_sm2"].inc(
                            sy.dma_start(out=cdb[:], in_=bc_ap(d_cd, 128, 128)), 16)
                        sy.wait_ge(S["s_dtt"].h, c + 1)
                        S["s_sm3"].inc(sy.dma_start(out=d_dtot, in_=dttc[:]), 16)
                        for hb in range(8):
                            sy.wait_ge(S["s_acso"].h, 16 * (c + 1))
                            if 8 * c + hb >= 1:
                                sy.wait_ge(S["s_mtd"].h, 8 * c + hb)
                            S["s_rbc"].inc(
                                sy.dma_start(
                                    out=rbcb[:]
                                    .rearrange("p (h l) -> p h l", h=16),
                                    in_=bass.AP(d_acs.tensor,
                                                (c * 128 + 16 * hb) * 128,
                                                [[0, 128], [128, 16], [1, 128]])),
                                16)
                            if hb >= 1:
                                sy.wait_ge(S["s_yev"].h, 8 * c + hb)
                                S["s_yst"].inc(
                                    sy.dma_start(
                                        out=d_yloc[c][:, 1024 * (hb - 1):
                                                      1024 * hb],
                                        in_=ystage[(hb - 1) % 2][:]), 16)
                        sy.wait_ge(S["s_yev"].h, 8 * c + 8)
                        S["s_yst"].inc(
                            sy.dma_start(out=d_yloc[c][:, 7168:8192],
                                         in_=ystage[7 % 2][:]), 16)
                    sy.wait_ge(S["s_yst"].h, 16 * 8 * NCH)
                    sy.wait_ge(S["s_acso"].h, 16 * NCH)
                    sy.wait_ge(S["s_sm1"].h, 16 * NCH)
                    sy.wait_ge(S["s_sm3"].h, 16 * NCH)
                    bar_out(sy)

                @blk.gpsimd
                def _(g):
                    bar_in(g)
                    g.memset(s_bf[:], 0.0)
                    for c in range(NCH):
                        g.wait_ge(S["s_sm3"].h, 16 * (c + 1))
                        g.wait_ge(S["s_sup"].h, 8 * c)
                        S["s_sm4"].inc(
                            g.dma_start(out=dtotb[:], in_=bc_ap(d_dtot, 128, 128)),
                            16)
                    bar_out(g)

                @blk.scalar
                def _(sc):
                    bar_in(sc)
                    S["s_dec"].inc(sc.activation(dec_fm[:, 0:1], alog_t[:], AF.Exp))
                    for c in range(NCH):
                        sc.wait_ge(S["s_dt1"].h, c + 1)
                        sc.activation(adt[:], sdt[:], AF.Abs)
                        sc.activation(edt[:], adt[:], AF.Exp, scale=-1.0)
                        S["s_dta"].inc(sc.activation(ldt[:], edt[:], AF.Ln,
                                                     bias=1.0))
                        sc.wait_ge(S["s_acs"].h, c + 1)
                        sc.activation(dec_fm[:], a_cs[:], AF.Exp, scale=-1.0,
                                      bias=a_cs[:, 127:128])
                        S["s_dec"].inc(sc.copy(dumpA[:], dec_fm[:]))
                        sc.wait_ge(S["s_cdu"].h, c)
                        if c > 0:
                            sc.wait_ge(S["s_sm1"].h, 16 * c)
                        S["s_cde"].inc(
                            sc.activation(cdxp[:], cdlog[c % 2][:], AF.Exp))
                        if c > 0:
                            sc.wait_ge(S["s_sm3"].h, 16 * c)
                        S["s_dtt"].inc(
                            sc.activation(dttc[:], a_cs[:, 127:128], AF.Exp))
                        for q in range(4):
                            sc.wait_ge(S["s_conv"].h, 4 * c + q + 1)
                            if q < 3:
                                S["s_silu"].inc(
                                    sc.activation(cs2[:, q * 2560:(q + 1) * 2560],
                                                  qb[:], AF.Silu))
                            else:
                                sc.activation(cs2[:, 7680:9216], qb[:, 0:1536],
                                              AF.Silu)
                                S["s_silu"].inc(
                                    sc.activation(
                                        cx_all[:, c * 1024:(c + 1) * 1024],
                                        qb[:, 1536:2560], AF.Silu))
                        sc.wait_ge(S["s_trevA"].h, c + 1)
                        sc.activation(sdoT_raw[:], a_csT[:], AF.Exp)
                        S["s_sdr"].inc(sc.copy(dumpA[:], sdoT_raw[:]))
                        for hb in range(8):
                            sc.wait_ge(S["s_mtd"].h, 8 * c + hb + 1)
                            sc.wait_ge(S["s_mt"].h, 8 * c + hb)
                            S["s_mte"].inc(
                                sc.activation(ebuf[:], dmc[:], AF.Exp))
                    bar_out(sc)

                @blk.tensor
                def _(t):
                    bar_in(t)
                    for c in range(NCH):
                        t.wait_ge(S["s_acs"].h, c + 1)
                        t.wait_ge(S["s_dec"].h, c + 2)
                        t.wait_ge(S["s_gmm"].h, c)
                        t.wait_ge(S["s_trevA"].h, c)
                        t.transpose(pAux[:, 0:128], a_cs[:], ident_f[:])
                        t.transpose(pAux[:, 128:256], dtc[:], ident_f[:])
                        ins = t.transpose(pAux[:, 256:384], dec_fm[:], ident_f[:])
                        S["s_trnA"].inc(ins)
                        for gi in range(9):
                            qreq = min(4, (8 * gi + 7) // 20 + 1)
                            t.wait_ge(S["s_silu"].h, 4 * c + qreq)
                            if gi == 0:
                                t.wait_ge(S["s_trevA"].h, c + 1)
                            if 9 * c + gi >= 2:
                                t.wait_ge(S["s_trevH"].h, 9 * c + gi - 1)
                            half = pAux_bf[:, 1024 * (gi % 2):1024 * (gi % 2) + 1024]
                            for j in range(8):
                                ins = t.transpose(
                                    half[:, j * 128:(j + 1) * 128],
                                    cs2[:, (8 * gi + j) * 128:(8 * gi + j + 1) * 128],
                                    ident_bf[:])
                            S["s_trnH"].inc(ins)
                        t.wait_ge(S["s_trevH"].h, 9 * c + 9)
                        t.wait_ge(S["s_silu"].h, 4 * c + 4)
                        for g8 in range(8):
                            ins = t.matmul(
                                pAux[:, g8 * 128:(g8 + 1) * 128],
                                cs2[:, (64 + g8) * 128:(65 + g8) * 128],
                                cx_all[:, c * 1024 + g8 * 128:
                                       c * 1024 + (g8 + 1) * 128],
                                start=True, stop=True)
                        S["s_gmt"].inc(ins)
                        for hb in range(8):
                            t.wait_ge(S["s_mt"].h, 8 * c + hb + 1)
                            t.wait_ge(S["s_yev"].h, 8 * c + hb)
                            t.wait_ge(S["s_trevH"].h, 9 * c + hb + 1)
                            for i in range(16):
                                h = 16 * hb + i
                                ins = t.matmul(
                                    pYd[:, i * 64:(i + 1) * 64],
                                    mtt[hb % 2][:, i * 128:(i + 1) * 128],
                                    x_tm[:, h * 64:(h + 1) * 64],
                                    start=True, stop=True)
                            S["s_yd"].inc(ins)
                            if c > 0:
                                t.wait_ge(S["s_sup"].h, 8 * (c - 1) + hb + 1)
                            for i in range(16):
                                h = 16 * hb + i
                                ins = t.matmul(
                                    pYo[:, i * 64:(i + 1) * 64],
                                    cx_all[:, c * 1024 + (i % 8) * 128:
                                           c * 1024 + (i % 8 + 1) * 128],
                                    s_bf[:, h * 64:(h + 1) * 64],
                                    start=True, stop=True)
                            S["s_yo"].inc(ins)
                            t.wait_ge(S["s_bd"].h, 8 * c + hb + 1)
                            t.wait_ge(S["s_sup"].h, 8 * c + hb)
                            for i in range(16):
                                h = 16 * hb + i
                                ins = t.matmul(
                                    pF[:, i * 64:(i + 1) * 64],
                                    bdt[hb % 2][:, i * 128:(i + 1) * 128],
                                    x_tm[:, h * 64:(h + 1) * 64],
                                    start=True, stop=True)
                            S["s_f"].inc(ins)
                    bar_out(t)

                @blk.vector
                def _(v):
                    bar_in(v)
                    v.wait_ge(S["s_dec"].h, 1)
                    v.wait_ge(S["s_tap"].h, 32)
                    v.tensor_scalar(acol[:], dec_fm[:, 0:1], -1.0, None, ALU.mult)

                    def conv_q(c, q):
                        v.wait_ge(S["s_hbcl"].h, 16 * (4 * c + q + 1))
                        wv = [cwv[:, 20 * q:20 * q + 20, k].unsqueeze(2)
                              .to_broadcast((128, 20, 128)) for k in range(KW)]
                        bv = (cb_bf[:].rearrange("p b -> p b")[:, 20 * q:20 * q + 20]
                              .unsqueeze(2).to_broadcast((128, 20, 128)))
                        qa3 = qa[:].rearrange("p (b t) -> p b t", b=20)
                        qb3 = qb[:].rearrange("p (b t) -> p b t", b=20)
                        qc3 = qc[:].rearrange("p (b t) -> p b t", b=20)
                        qd3 = qd[:].rearrange("p (b t) -> p b t", b=20)
                        v.tensor_tensor(qa3, hview[:, :, 0:128], wv[0], ALU.mult)
                        if 4 * c + q >= 1:
                            v.wait_ge(S["s_silu"].h, 4 * c + q)
                        v.tensor_tensor(qb3, hview[:, :, 1:129], wv[1], ALU.mult)
                        v.tensor_tensor(qc3, qa3, qb3, ALU.add)
                        v.tensor_tensor(qa3, hview[:, :, 2:130], wv[2], ALU.mult)
                        v.tensor_tensor(qb3, hview[:, :, 3:131], wv[3], ALU.mult)
                        v.tensor_tensor(qd3, qa3, qb3, ALU.add)
                        v.tensor_tensor(qa3, qc3, qd3, ALU.add)
                        S["s_conv"].inc(v.tensor_tensor(qb3, qa3, bv, ALU.add))

                    def tail(c, b):
                        v.wait_ge(S["s_mte"].h, 8 * c + b + 1)
                        if 8 * c + b >= 2:
                            v.wait_ge(S["s_yd"].h, 8 * c + b - 1)
                        ev = ebuf[:].rearrange("s (r g l) -> s r g l", r=2, g=8)
                        gv = (gmm[:].rearrange("s (g l) -> s g l", g=8)
                              .unsqueeze(1).to_broadcast((128, 2, 8, 128)))
                        S["s_mt"].inc(
                            v.tensor_tensor(
                                mtt[b % 2][:].rearrange("s (r g l) -> s r g l",
                                                        r=2, g=8),
                                ev, gv, ALU.mult))
                        if 8 * c + b >= 2:
                            v.wait_ge(S["s_f"].h, 8 * c + b - 1)
                        bxv = (bx_tm[:].rearrange("s (g n) -> s g n", g=8)
                               .unsqueeze(1).to_broadcast((128, 2, 8, 128)))
                        dcv = (decT_bf[:, 16 * b:16 * b + 16]
                               .rearrange("s (r g) -> s r g", r=2)
                               .unsqueeze(3).to_broadcast((128, 2, 8, 128)))
                        S["s_bd"].inc(
                            v.tensor_tensor(
                                bdt[b % 2][:].rearrange("s (r g n) -> s r g n",
                                                        r=2, g=8),
                                bxv, dcv, ALU.mult))
                        v.wait_ge(S["s_yd"].h, 8 * c + b + 1)
                        v.wait_ge(S["s_yo"].h, 8 * c + b + 1)
                        if 8 * c + b >= 2:
                            v.wait_ge(S["s_yst"].h, 16 * (8 * c + b - 1))
                        sv = (sdo_all[:, c * 128 + 16 * b:c * 128 + 16 * b + 16]
                              .unsqueeze(2).to_broadcast((128, 16, 64)))
                        v.tensor_tensor(
                            tev1[:].rearrange("l (h p) -> l h p", h=16),
                            pYo[:].rearrange("l (h p) -> l h p", h=16),
                            sv, ALU.mult)
                        v.tensor_tensor(tev2[:], tev1[:], pYd[:], ALU.add)
                        dv = (ddtT_bf[:, 16 * b:16 * b + 16]
                              .unsqueeze(2).to_broadcast((128, 16, 64)))
                        v.tensor_tensor(
                            dres[:].rearrange("s (h p) -> s h p", h=16),
                            x_tm[:, 1024 * b:1024 * (b + 1)]
                            .rearrange("s (h p) -> s h p", h=16),
                            dv, ALU.mult)
                        S["s_yev"].inc(
                            v.tensor_tensor(ystage[b % 2][:], tev2[:], dres[:],
                                            ALU.add))
                        v.wait_ge(S["s_f"].h, 8 * c + b + 1)
                        v.wait_ge(S["s_sm4"].h, 16 * (c + 1))
                        dtv = (dtotb[:, 16 * b:16 * b + 16]
                               .unsqueeze(2).to_broadcast((128, 16, 64)))
                        v.tensor_tensor(
                            tev1[:].rearrange("n (h p) -> n h p", h=16),
                            s_bf[:, 1024 * b:1024 * (b + 1)]
                            .rearrange("n (h p) -> n h p", h=16),
                            dtv, ALU.mult)
                        S["s_sup"].inc(
                            v.tensor_tensor(s_bf[:, 1024 * b:1024 * (b + 1)],
                                            tev1[:], pF[:], ALU.add))

                    for c in range(NCH):
                        T0 = 3 + 128 * c
                        v.tensor_scalar(sdt[:], dt_raw[:, T0:T0 + 128], dtb_t[:],
                                        None, ALU.add)
                        S["s_dt1"].inc(v.tensor_copy(dumpV[:], sdt[:]))
                        conv_q(c, 0)
                        v.wait_ge(S["s_dta"].h, c + 1)
                        v.tensor_scalar(mdt[:], sdt[:], 0.0, None, ALU.max)
                        v.tensor_tensor(dtc[:], mdt[:], ldt[:], ALU.add)
                        v.tensor_scalar(admt[:], dtc[:], acol[:], None, ALU.mult)
                        if c > 0:
                            v.wait_ge(S["s_acso"].h, 16 * c)
                        v.tensor_tensor_scan(a_cs[:], ones128[:], admt[:],
                                             0.0, ALU.mult, ALU.add)
                        v.tensor_copy(dumpV[:], a_cs[:])
                        S["s_acs"].inc(v.tensor_copy(dumpV[:], dtc[:]))
                        v.wait_ge(S["s_cde"].h, c)
                        S["s_cdu"].inc(
                            v.tensor_tensor(cdlog[(c + 1) % 2][:], cdlog[c % 2][:],
                                            a_cs[:, 127:128], ALU.add))
                        conv_q(c, 1)
                        v.wait_ge(S["s_trnA"].h, c + 1)
                        v.tensor_copy(a_csT[:], pAux[:, 0:128])
                        v.tensor_copy(dtT_f[:], pAux[:, 128:256])
                        v.tensor_copy(dtT_bf[:], pAux[:, 128:256])
                        v.tensor_copy(decT_bf[:], pAux[:, 256:384])
                        v.reciprocal(rdt[:], dtT_f[:])
                        v.tensor_copy(dumpV[:], trimask[:])
                        v.tensor_copy(dumpV[:], ones128[:])
                        v.tensor_tensor(ddtT_bf[:], rdt[:], dbc[:], ALU.mult)
                        S["s_trevA"].inc(v.tensor_copy(dumpV[:], a_csT[:]))
                        conv_q(c, 2)
                        conv_q(c, 3)
                        v.wait_ge(S["s_sdr"].h, c + 1)
                        v.wait_ge(S["s_sm2"].h, 16 * (c + 1))
                        S["s_sdo"].inc(
                            v.tensor_tensor(sdo_all[:, c * 128:(c + 1) * 128],
                                            sdoT_raw[:], cdb[:], ALU.mult))
                        for gi in range(9):
                            v.wait_ge(S["s_trnH"].h, 9 * c + gi + 1)
                            half = pAux_bf[:, 1024 * (gi % 2):
                                           1024 * (gi % 2) + 1024]
                            if gi < 8:
                                dtv2 = (dtT_bf[:, 16 * gi:16 * gi + 16]
                                        .rearrange("s (j h2) -> s j h2", j=8)
                                        .unsqueeze(3)
                                        .to_broadcast((128, 8, 2, 64)))
                                S["s_trevH"].inc(
                                    v.tensor_tensor(
                                        x_tm[:, gi * 1024:(gi + 1) * 1024]
                                        .rearrange("s (j h2 p) -> s j h2 p",
                                                   j=8, h2=2),
                                        half[:].rearrange(
                                            "s (j h2 p) -> s j h2 p",
                                            j=8, h2=2),
                                        dtv2, ALU.mult))
                            else:
                                S["s_trevH"].inc(
                                    v.tensor_copy(bx_tm[:], half[:]))
                        v.wait_ge(S["s_gmt"].h, c + 1)
                        tv = (trimask[:].unsqueeze(1)
                              .to_broadcast((128, 8, 128)))
                        S["s_gmm"].inc(
                            v.tensor_tensor(
                                gmm[:].rearrange("s (g l) -> s g l", g=8),
                                pAux[:].rearrange("s (g l) -> s g l", g=8),
                                tv, ALU.mult))
                        for hb in range(8):
                            v.wait_ge(S["s_rbc"].h, 16 * (8 * c + hb + 1))
                            v.wait_ge(S["s_mte"].h, 8 * c + hb)
                            av = (a_csT[:, 16 * hb:16 * hb + 16]
                                  .unsqueeze(2).to_broadcast((128, 16, 128)))
                            v.tensor_tensor(
                                dmt[:].rearrange("s (h l) -> s h l", h=16),
                                rbcb[:].rearrange("s (h l) -> s h l", h=16),
                                av, ALU.subtract)
                            S["s_mtd"].inc(
                                v.tensor_scalar(dmc[:], dmt[:], 0.0, None,
                                                ALU.min))
                            if hb > 0:
                                tail(c, hb - 1)
                        tail(c, 7)
                    bar_out(v)

        phase[0] += 1

        if STOP_AFTER == "B":
            return
        with ExitStack() as sG:
            e3 = sG.enter_context
            Et = [e3(nc.sbuf_tensor(f"Et{i}", [128, INTER], F32)) for i in range(2)]
            Gt = [e3(nc.sbuf_tensor(f"Gt{i}", [128, INTER], F32)) for i in range(2)]
            gdl = [e3(nc.sbuf_tensor(f"gdl{i}", [128, 1], F32)) for i in range(2)]
            dsf = e3(nc.sbuf_tensor("dsf", [128, 1], F32))
            dlb = [e3(nc.sbuf_tensor(f"dlb{i}", [128, 128], F32)) for i in range(2)]
            dxc = e3(nc.sbuf_tensor("dxc", [128, 1], F32))
            dxb = e3(nc.sbuf_tensor("dxb", [128, 1], BF16))
            E_bf = s_bf

            with nc.Block("passG") as blk:

                @blk.scalar
                def _(sc):
                    bar_in(sc)
                    sc.activation(dxc[:], cdlog[NCH % 2][:], AF.Exp)
                    S["s_dstg"].inc(sc.copy(gdl[0][:], dxc[:]))
                    bar_out(sc)

                @blk.vector
                def _(v):
                    bar_in(v)
                    v.wait_ge(S["s_dstg"].h, 1)
                    S["s_dstg"].inc(v.tensor_copy(dxb[:], dxc[:]))
                    v.memset(Et[0][:], 0.0)
                    for j in range(NCORE - 1):
                        v.wait_ge(S["s_gt1" if j % 2 else "s_gt"].h,
                                  32 * (j // 2 + 1))
                        S["s_dstg"].inc(
                            v.tensor_scalar(dsf[:], gdl[j % 2][:],
                                            lm_t[:, j:j + 1],
                                            ilm_t[:, j:j + 1],
                                            ALU.mult, ALU.add))
                        v.wait_ge(S["s_dbc"].h, 32 * (j + 1))
                        dl = (dlb[j % 2][:].unsqueeze(2)
                              .to_broadcast((128, 128, 64)))
                        v.tensor_tensor(
                            Et[1][:].rearrange("n (h p) -> n h p", h=128),
                            Et[0][:].rearrange("n (h p) -> n h p", h=128),
                            dl, ALU.mult)
                        S["s_ecomb"].inc(
                            v.scalar_tensor_tensor(Et[0][:], Gt[j % 2][:],
                                                   lm_t[:, j:j + 1], Et[1][:],
                                                   ALU.mult, ALU.add))
                    v.wait_ge(S["s_blob"].h, 32)
                    if d_sbf is not None:
                        v.wait_ge(S["s_tap"].h, 48)
                    v.tensor_copy(E_bf[:], Et[0][:])
                    bar_out(v)

                @blk.sync
                def _(sy):
                    bar_in(sy)
                    sy.wait_ge(S["s_dstg"].h, 2)
                    S["s_blob"].inc(
                        sy.dma_start(out=cc_in[:, 0:INTER], in_=s_bf[:]), 16)
                    S["s_blob"].inc(
                        sy.dma_start(out=cc_in[:, INTER:INTER + 1], in_=dxb[:],
                                     allow_slow_non_contiguous=True), 16)
                    for j in range(NCORE - 1):
                        sy.wait_ge(S["s_dstg"].h, j + 3)
                        S["s_dbc"].inc(sy.dma_start(out=d_dst, in_=dsf[:]), 16)
                        if j >= 2:
                            sy.wait_ge(S["s_ecomb"].h, j - 1)
                        S["s_dbc"].inc(
                            sy.dma_start(out=dlb[j % 2][:],
                                         in_=bc_ap(d_dst, 128, 128)), 16)
                    sy.wait_ge(S["s_dbc"].h, 32 * (NCORE - 1))
                    if d_cco is not None:
                        sy.wait_ge(S["s_cc2"].h, 1)
                        S["s_tap"].inc(sy.dma_start(out=d_cco, in_=cc_out), 16)
                    if d_sbf is not None:
                        S["s_tap"].inc(sy.dma_start(out=d_sbf, in_=s_bf[:]), 16)
                    if d_ebf is not None:
                        sy.wait_ge(S["s_ecomb"].h, NCORE - 1)
                        S["s_tap"].inc(sy.dma_start(out=d_ebf, in_=E_bf[:]), 16)
                    bar_out(sy)

                @blk.gpsimd
                def _(g):
                    bar_in(g)
                    g.wait_ge(S["s_blob"].h, 32)
                    S["s_cc2"].inc(
                        g.collective_compute(
                            "AllGather", ALU.bypass,
                            replica_groups=[list(range(NCORE))],
                            ins=[cc_in], outs=[cc_out]))
                    g.wait_ge(S["s_cc2"].h, 1)
                    for j in range(NCORE - 1):
                        if j >= 2:
                            g.wait_ge(S["s_ecomb"].h, j - 1)
                        gsem = S["s_gt1"] if j % 2 else S["s_gt"]
                        gsem.inc(
                            g.dma_start(out=Gt[j % 2][:],
                                        in_=cc_out[j * 128:(j + 1) * 128,
                                                   0:INTER]), 16)
                        gsem.inc(
                            g.dma_start(out=gdl[j % 2][:],
                                        in_=cc_out[j * 128:(j + 1) * 128,
                                                   INTER:INTER + 1],
                                        allow_slow_non_contiguous=True), 16)
                    bar_out(g)

                @blk.tensor
                def _(t):
                    bar_in(t)
                    bar_out(t)

        phase[0] += 1
        env["E_bf"] = E_bf
        env["cx_all"] = cx_all
        env["sdo_all"] = sdo_all
        if STOP_AFTER not in ("B", "G"):
            _build_c(nc, env)
    if STOP_AFTER not in ("B", "G", "C"):
        _build_d(nc, env)


def _build_c(nc, env):
    S = env["S"]
    bar_in, bar_out = env["bar_in"], env["bar_out"]
    phase = env["phase"]
    bc_ap = env["bc_ap"]
    T, NCH = env["T"], env["NCH"]
    pYd, pYo, pAux = env["pYd"], env["pYo"], env["pAux"]
    ident_bf = env["ident_bf"]
    d_gate, d_yloc, d_norm = env["d_gate"], env["d_yloc"], env["d_norm"]
    nwv = env["nwv"]
    cx_all, sdo_all, E_bf = env["cx_all"], env["sdo_all"], env["E_bf"]
    pAux_bf = pAux[:].bitcast(BF16)

    with ExitStack() as sC:
        ec = sC.enter_context
        ybuf = ec(nc.sbuf_tensor("ybuf", [128, INTER], F32))
        gbuf = ec(nc.sbuf_tensor("gbuf", [128, INTER], BF16))
        gate_tm = ec(nc.sbuf_tensor("gate_tm", [128, INTER], BF16))
        xg = ec(nc.sbuf_tensor("xg", [128, INTER], BF16))
        nw_bf = ec(nc.sbuf_tensor("nw_bf", [128, INTER], BF16))
        normed_tm = ec(nc.sbuf_tensor("normed_tm", [128, INTER], BF16))
        ctv1 = ec(nc.sbuf_tensor("ctv1", [128, 1024], F32))
        ctv2 = ec(nc.sbuf_tensor("ctv2", [128, 1024], BF16))
        sqd = ec(nc.sbuf_tensor("sqd", [128, 1024], BF16))
        ms8 = ec(nc.sbuf_tensor("ms8", [128, 8], F32))
        msv = ec(nc.sbuf_tensor("msv", [128, 128], F32))
        l8 = ec(nc.sbuf_tensor("l8", [128, 128], F32))
        rstd8 = ec(nc.sbuf_tensor("rstd8", [128, 128], F32))



        with nc.Block("passC") as blk:

            @blk.sync
            def _(sy):
                bar_in(sy)
                for c in range(NCH):
                    T0 = 3 + 128 * c
                    if c >= 1:
                        sy.wait_ge(S["s_cev"].h, 8 * c)
                    S["s_ylc"].inc(sy.dma_start(out=ybuf[:], in_=d_yloc[c]), 16)
                    if c >= 1:
                        sy.wait_ge(S["s_nrm"].h, c)
                    S["s_glc"].inc(
                        sy.dma_start(
                            out=gbuf[:].rearrange("p (b t) -> p b t", b=64),
                            in_=d_gate[:, :, T0:T0 + 128]
                            .rearrange("b p t -> p b t")), 16)
                    sy.wait_ge(S["s_nrm"].h, c + 1)
                    S["s_nst"].inc(sy.dma_start(out=d_norm[c], in_=normed_tm[:]),
                                   16)
                sy.wait_ge(S["s_nst"].h, 16 * NCH)
                bar_out(sy)

            @blk.gpsimd
            def _(g):
                bar_in(g)
                g.memset(msv[:], 1.0)
                S["s_nwl"].inc(
                    g.dma_start(out=nw_bf[:], in_=bc_ap(nwv, 128, INTER)), 16)
                bar_out(g)

            @blk.tensor
            def _(t):
                bar_in(t)
                for c in range(NCH):
                    for gi in range(8):
                        t.wait_ge(S["s_glc"].h, 16 * (c + 1))
                        if 8 * c + gi >= 2:
                            t.wait_ge(S["s_gte"].h, 8 * c + gi - 1)
                        half = pAux_bf[:, 1024 * (gi % 2):1024 * (gi % 2) + 1024]
                        for j in range(8):
                            ins = t.transpose(
                                half[:, j * 128:(j + 1) * 128],
                                gbuf[:, (8 * gi + j) * 128:(8 * gi + j + 1) * 128],
                                ident_bf[:])
                        S["s_gtr"].inc(ins)
                    for hb in range(8):
                        pC = [pYd, pYo][hb % 2]
                        if 8 * c + hb >= 2:
                            t.wait_ge(S["s_cev"].h, 8 * c + hb - 1)
                        for i in range(16):
                            h = 16 * hb + i
                            ins = t.matmul(
                                pC[:, i * 64:(i + 1) * 64],
                                cx_all[:, c * 1024 + (i % 8) * 128:
                                       c * 1024 + (i % 8 + 1) * 128],
                                E_bf[:, h * 64:(h + 1) * 64],
                                start=True, stop=True)
                        S["s_corr"].inc(ins)
                bar_out(t)

            @blk.scalar
            def _(sc):
                bar_in(sc)
                sc.wait_ge(S["s_nwl"].h, 16)
                for c in range(NCH):
                    for gi in range(8):
                        sc.wait_ge(S["s_gtr"].h, 8 * c + gi + 1)
                        half = pAux_bf[:, 1024 * (gi % 2):1024 * (gi % 2) + 1024]
                        S["s_gte"].inc(
                            sc.copy(gate_tm[:, gi * 1024:(gi + 1) * 1024],
                                    half[:]))
                    sc.wait_ge(S["s_rms1"].h, c + 1)
                    sc.activation(l8[:], msv[:], AF.Ln)
                    sc.activation(rstd8[:], l8[:], AF.Exp, scale=-0.5)
                    S["s_rmsa"].inc(sc.copy(l8[:], rstd8[:]))
                bar_out(sc)

            @blk.vector
            def _(v):
                bar_in(v)
                v.wait_ge(S["s_nwl"].h, 16)
                for c in range(NCH):
                    for hb in range(8):
                        v.wait_ge(S["s_corr"].h, 8 * c + hb + 1)
                        v.wait_ge(S["s_gte"].h, 8 * c + hb + 1)
                        v.wait_ge(S["s_ylc"].h, 16 * (c + 1))
                        pC = [pYd, pYo][hb % 2]
                        sv = (sdo_all[:, c * 128 + 16 * hb:c * 128 + 16 * hb + 16]
                              .unsqueeze(2).to_broadcast((128, 16, 64)))
                        v.tensor_tensor(
                            ctv1[:].rearrange("l (h p) -> l h p", h=16),
                            pC[:].rearrange("l (h p) -> l h p", h=16),
                            sv, ALU.mult)
                        v.tensor_tensor(ctv2[:], ctv1[:],
                                        ybuf[:, 1024 * hb:1024 * (hb + 1)],
                                        ALU.add)
                        S["s_cev"].inc(
                            v.tensor_tensor(xg[:, 1024 * hb:1024 * (hb + 1)],
                                            ctv2[:],
                                            gate_tm[:, 1024 * hb:1024 * (hb + 1)],
                                            ALU.mult))
                    for g2 in range(8):
                        v.tensor_tensor(sqd[:],
                                        xg[:, 1024 * g2:1024 * (g2 + 1)],
                                        xg[:, 1024 * g2:1024 * (g2 + 1)],
                                        ALU.mult)
                        v.reduce_sum(ms8[:, g2:g2 + 1], sqd[:], axis=AX.X)
                    v.tensor_copy(sqd[:], xg[:, 0:1024])
                    v.tensor_copy(sqd[:], xg[:, 1024:2048])
                    v.tensor_scalar(msv[:, 0:8], ms8[:], 1.0 / 1024.0,
                                    RMS_EPS, ALU.mult, ALU.add)
                    S["s_rms1"].inc(v.tensor_copy(sqd[:, 0:8], msv[:, 0:8]))
                    v.wait_ge(S["s_rmsa"].h, c + 1)
                    v.wait_ge(S["s_gtr"].h, 8 * c + 8)
                    if c >= 1:
                        v.wait_ge(S["s_nst"].h, 16 * c)
                    for g2 in range(8):
                        v.tensor_scalar(gbuf[:, 1024 * g2:1024 * (g2 + 1)],
                                        xg[:, 1024 * g2:1024 * (g2 + 1)],
                                        rstd8[:, g2:g2 + 1], None, ALU.mult)
                    S["s_nrm"].inc(
                        v.tensor_tensor(normed_tm[:], gbuf[:], nw_bf[:],
                                        ALU.mult))
                bar_out(v)

    phase[0] += 1


def _build_d(nc, env):
    S = env["S"]
    bar_in, bar_out = env["bar_in"], env["bar_out"]
    phase = env["phase"]
    T, tok = env["T"], env["tok"]
    TSW, NTS, NSUB = env["TSW"], env["NTS"], env["NSUB"]
    pYd, pYo = env["pYd"], env["pYo"]
    x_fm, wd, out = env["x_fm"], env["wd"], env["out"]
    d_ffnb, d_norm = env["d_ffnb"], env["d_norm"]
    t_sc, rr_col = env["t_sc"], env["rr_col"]

    pslots = [pYd[:, 0:512], pYd[:, 512:1024], pYo[:, 0:512], pYo[:, 512:1024]]

    with ExitStack() as sD:
        ed = sD.enter_context
        rhs_n = ed(nc.sbuf_tensor("rhs_n", [128, KB_D * TSW], BF16))
        wdbuf = [ed(nc.sbuf_tensor(f"wdbuf{i}", [128, KB_D * 128], BF16))
                 for i in range(2)]
        wlora = [ed(nc.sbuf_tensor(f"wlora{i}", [128, 128], BF16))
                 for i in range(2)]
        xbuf = ed(nc.sbuf_tensor("xbuf", [128, TSW], F32))
        fbuf = ed(nc.sbuf_tensor("fbuf", [128, TSW], F32))
        ostage = ed(nc.sbuf_tensor("ostage", [128, TSW], F32))
        dtv = ed(nc.sbuf_tensor("dtv", [128, 512], F32))

        with nc.Block("passD") as blk:

            @blk.sync
            def _(sy):
                bar_in(sy)
                for ts in range(NTS):
                    if ts >= 1:
                        sy.wait_ge(S["s_dmm"].h, 32 * NSUB * ts)
                    for kb in range(KB_D - 1):
                        S[f"s_rh{kb // 16}"].inc(
                            sy.dma_start_transpose(
                                rhs_n[:, kb * TSW:(kb + 1) * TSW],
                                d_norm.rearrange("c s f -> (c s) f")
                                [ts * TSW:(ts + 1) * TSW,
                                 kb * 128:(kb + 1) * 128]), 16)
                    for db in range(32):
                        gidx = 32 * ts + db
                        sy.wait_ge(S["s_dev"].h, NSUB * gidx)
                        S["s_xb"].inc(
                            sy.dma_start(
                                out=xbuf[:],
                                in_=x_fm[db * 128:(db + 1) * 128,
                                         3 + ts * TSW:3 + (ts + 1) * TSW]), 16)
                        if db >= 1:
                            sy.wait_ge(S["s_dev"].h, NSUB * gidx)
                            S["s_out"].inc(
                                sy.dma_start(
                                    out=out[db - 1][:, ts * TSW:(ts + 1) * TSW],
                                    in_=ostage[:]), 16)
                    sy.wait_ge(S["s_dev"].h, NSUB * (32 * ts + 32))
                    S["s_out"].inc(
                        sy.dma_start(out=out[31][:, ts * TSW:(ts + 1) * TSW],
                                     in_=ostage[:]), 16)
                sy.wait_ge(S["s_out"].h, 16 * 32 * NTS)
                bar_out(sy)

            @blk.gpsimd
            def _(g):
                bar_in(g)
                for ts in range(NTS):
                    for db in range(32):
                        gidx = 32 * ts + db
                        if gidx >= 2:
                            g.wait_ge(S["s_dmm"].h, NSUB * (gidx - 1))
                        wdsem = S["s_wd1"] if gidx % 2 else S["s_wd"]
                        wdsem.inc(
                            g.dma_start(
                                out=wdbuf[db % 2][:],
                                in_=wd.rearrange("(kb p) d -> p kb d", p=128)
                                [:, :, db * 128:(db + 1) * 128]), 16)
                        g.wait_ge(S["s_dev"].h, NSUB * gidx)
                        S["s_fb"].inc(
                            g.dma_start(
                                out=fbuf[:],
                                in_=d_ffnb[db][:, 3 + ts * TSW:
                                               3 + (ts + 1) * TSW]), 16)
                bar_out(g)

            @blk.tensor
            def _(t):
                bar_in(t)
                for ts in range(NTS):
                    for db in range(32):
                        gidx = 32 * ts + db
                        t.wait_ge(S["s_wd1" if gidx % 2 else "s_wd"].h,
                                  16 * (gidx // 2 + 1))
                        t.wait_ge(S["s_wl"].h, gidx + 1)
                        for tsub in range(NSUB):
                            k = NSUB * gidx + tsub
                            if k >= 4:
                                t.wait_ge(S["s_dev"].h, k - 3)
                            ps = pslots[k % 4]
                            for kb in range(KB_D):
                                if db == 0 and tsub == 0 and kb % 16 == 0 \
                                        and kb < 64:
                                    t.wait_ge(S[f"s_rh{kb // 16}"].h,
                                              16 * 16 * (ts + 1))
                                if kb == KB_D - 1:
                                    lhsT = wlora[db % 2][:]
                                    rhs = t_sc[:, 3 + ts * TSW + tsub * 512:
                                               3 + ts * TSW + tsub * 512 + 512]
                                else:
                                    lhsT = wdbuf[db % 2][:, kb * 128:(kb + 1) * 128]
                                    rhs = rhs_n[:, kb * TSW + tsub * 512:
                                                kb * TSW + tsub * 512 + 512]
                                ins = t.matmul(ps[:, 0:512], lhsT, rhs,
                                               start=(kb == 0),
                                               stop=(kb == KB_D - 1))
                            S["s_dmm"].inc(ins)
                bar_out(t)

            @blk.scalar
            def _(sc):
                bar_in(sc)
                bar_out(sc)

            @blk.vector
            def _(v):
                bar_in(v)
                for ts in range(NTS):
                    for db in range(32):
                        gidx = 32 * ts + db
                        v.wait_ge(S["s_wd1" if gidx % 2 else "s_wd"].h,
                                  16 * (gidx // 2 + 1))
                        if gidx >= 2:
                            v.wait_ge(S["s_dmm"].h, NSUB * (gidx - 1))
                        v.tensor_scalar(
                            wlora[db % 2][:],
                            wdbuf[db % 2][:, (KB_D - 1) * 128:KB_D * 128],
                            rr_col[:], None, ALU.mult)
                        S["s_wl"].inc(
                            v.tensor_copy(dtv[:, 0:64],
                                          wlora[db % 2][:].bitcast(F32)))
                        for tsub in range(NSUB):
                            k = NSUB * gidx + tsub
                            v.wait_ge(S["s_dmm"].h, k + 1)
                            v.wait_ge(S["s_xb"].h, 16 * (gidx + 1))
                            v.wait_ge(S["s_fb"].h, 16 * (gidx + 1))
                            if gidx >= 1 and tsub == 0:
                                v.wait_ge(S["s_out"].h, 16 * gidx)
                            ps = pslots[k % 4]
                            v.tensor_tensor(dtv[:], ps[:, 0:512],
                                            xbuf[:, tsub * 512:(tsub + 1) * 512],
                                            ALU.add)
                            S["s_dev"].inc(
                                v.tensor_tensor(
                                    ostage[:, tsub * 512:(tsub + 1) * 512],
                                    dtv[:],
                                    fbuf[:, tsub * 512:(tsub + 1) * 512],
                                    ALU.add))
                bar_out(v)

    phase[0] += 1


def _prep(inputs, tok):
    T = tok + 3
    f32 = np.float32
    x = np.asarray(inputs["L0_attn_hidden_states"], f32)[0]
    in_w = np.asarray(inputs["L0_attn_in_proj_w"], f32)
    conv_w = np.asarray(inputs["L0_attn_conv_w"], f32)[:, 0, :]
    conv_b = np.asarray(inputs["L0_attn_conv_b"], f32)
    dt_bias = np.asarray(inputs["L0_attn_dt_bias"], f32)
    A_log = np.asarray(inputs["L0_attn_A_log"], f32)
    Dv = np.asarray(inputs["L0_attn_D"], f32)
    nw = np.asarray(inputs["L0_attn_norm_weight"], f32)
    out_w = np.asarray(inputs["L0_attn_out_proj_w"], f32)
    fW = np.asarray(inputs["L0_ffn_W"], f32)
    fb = np.asarray(inputs["L0_ffn_b"], f32)
    fA = np.asarray(inputs["L0_ffn_A"], f32)
    fB = np.asarray(inputs["L0_ffn_B"], f32)
    rW = np.asarray(inputs["L0_ffn_router_W"], f32)
    rb = np.asarray(inputs["L0_ffn_router_b"], f32)

    ncore = NCORE
    xt = np.ascontiguousarray(x.T)
    router_pad = np.zeros((HID, 128), f32)
    router_pad[:, :8] = rW.T
    WA = np.ascontiguousarray(np.concatenate(
        [in_w.T, fA.reshape(128, HID).T, router_pad, fW.T], axis=1))
    WD = np.ascontiguousarray(np.concatenate(
        [out_w.T, fB.transpose(0, 2, 1).reshape(128, HID)], axis=0))
    cwP = np.ascontiguousarray(
        conv_w.reshape(NB_HBC, 128, KW).transpose(1, 0, 2).reshape(128, NB_HBC * KW))
    cbP = np.ascontiguousarray(conv_b.reshape(NB_HBC, 128).T)
    bpP = np.ascontiguousarray(fb.reshape(ND, 128).T)

    in_maps = []
    for i in range(ncore):
        s0 = i * tok
        xi = np.zeros((HID, T), f32)
        lo = max(0, s0 - 3)
        xi[:, 3 - (s0 - lo):] = xt[:, lo:s0 + tok]
        rm = np.zeros((1, T), f32)
        for tloc in range(3, T):
            gtok = s0 + tloc - 3
            if Q_LO <= gtok < Q_HI:
                rm[0, tloc] = 1.0 / NQ
        rm8 = np.broadcast_to(rm, (8, T)).copy()
        lmv = np.zeros((128, 8), f32)
        lmv[:, :i] = 1.0
        in_maps.append({
            "x": np.ascontiguousarray(xi),
            "wa": WA, "wd": WD, "cw": cwP, "cb": cbP,
            "dtb": np.ascontiguousarray(dt_bias.reshape(128, 1)),
            "alog": np.ascontiguousarray(A_log.reshape(128, 1)),
            "dv": np.ascontiguousarray(Dv.reshape(128, 1)),
            "nw": np.ascontiguousarray(nw.reshape(1, INTER)),
            "bp": bpP,
            "rb": np.ascontiguousarray(rb.reshape(1, 8)),
            "rmask": rm8,
            "lmask": np.ascontiguousarray(lmv),
            "ilmask": np.ascontiguousarray(1.0 - lmv),
        })
    return in_maps


_NC_CACHE = {}


def kernel(**inputs):
    tok = SEQ // NCORE
    in_maps = _prep(inputs, tok)
    key = tok
    if key not in _NC_CACHE:
        _NC_CACHE[key] = build(tok=tok)
    nc = _NC_CACHE[key]
    res = run_bass_kernel_spmd(nc, in_maps, core_ids=list(range(NCORE)))
    full = np.empty((SEQ, HID), np.float32)
    for i in range(NCORE):
        o = np.asarray(res.results[i]["out"])
        full[i * tok:(i + 1) * tok, :] = o.reshape(HID, tok).T
    return full[None]
